# version 23
# baseline (speedup 1.0000x reference)
import numpy as np
from contextlib import ExitStack

import concourse.bass as bass
import concourse.mybir as mybir
from concourse.bass_utils import run_bass_kernel_spmd

F32 = mybir.dt.float32
BF16 = mybir.dt.bfloat16
AF = mybir.ActivationFunctionType
ALU = mybir.AluOpType
AX = mybir.AxisListType.X

NCORES = 8
NB = 2
S = 4096
D = 1024
H = 16
NEG = -30000.0
BCONST = 32.0
NT = S // 128
NST = S // 512

DBG = False


class Sem:
    def __init__(self, h):
        self.h = h
        self.n = 0


class Prog:
    ENGS = ("sync", "tensor", "vector", "scalar", "gpsimd")

    def __init__(self, nc, es):
        self.nc = nc
        self.es = es
        self.cur = None

    def sem(self, name):
        return Sem(self.es.enter_context(self.nc.semaphore(name)))

    def new_block(self):
        assert self.cur is None
        self.cur = {e: [] for e in self.ENGS}

    def op(self, eng, fn, inc=None):
        if inc is not None:
            inc[0].n += inc[1]
        self.cur[eng].append((fn, inc))

    def w(self, eng, sem, v=None):
        val = sem.n if v is None else v
        if val <= 0:
            return
        self.cur[eng].append((lambda e, s=sem.h, x=val: e.wait_ge(s, x), None))

    def flush(self):
        blkops = self.cur
        self.cur = None
        with self.nc.Block() as blk:
            for eng in self.ENGS:
                ops = blkops[eng]
                if not ops:
                    continue
                def body(e, ops=ops):
                    for fn, inc in ops:
                        r = fn(e)
                        if inc is not None:
                            r.then_inc(inc[0].h, inc[1])
                getattr(blk, eng)(body)


def build(dbg=DBG):
    nc = bass.Bass("TRN2", target_bir_lowering=False)
    dt_scr = "ExternalOutput" if dbg else "Internal"

    x_d = nc.dram_tensor("x", [NB, S, D], F32, kind="ExternalInput").ap()
    wq_d = nc.dram_tensor("wq", [128, 8 * D], F32, kind="ExternalInput").ap()
    wk_d = nc.dram_tensor("wk", [128, 8 * D], F32, kind="ExternalInput").ap()
    wv_d = nc.dram_tensor("wv", [128, 8 * D], F32, kind="ExternalInput").ap()
    bq_d = nc.dram_tensor("bq", [8, 128], F32, kind="ExternalInput").ap()
    bk_d = nc.dram_tensor("bk", [8, 128], F32, kind="ExternalInput").ap()
    bv_d = nc.dram_tensor("bv", [1, D], F32, kind="ExternalInput").ap()
    gw_d = nc.dram_tensor("gw", [128, 64], F32, kind="ExternalInput").ap()
    ew_d = nc.dram_tensor("ew", [8, 128, 8 * D], F32, kind="ExternalInput").ap()
    c_d = nc.dram_tensor("c2", [NB, D], F32, kind="ExternalInput").ap()
    wm_d = nc.dram_tensor("wm", [48, 128, D], F32, kind="ExternalInput").ap()
    idb_d = nc.dram_tensor("idb", [128, 128], BF16, kind="ExternalInput").ap()
    idf_d = nc.dram_tensor("idf", [128, 128], F32, kind="ExternalInput").ap()
    kpad2_d = nc.dram_tensor("kpad2", [2, 128], BF16, kind="ExternalInput").ap()
    kpadB_d = nc.dram_tensor("kpadB", [2, 384], BF16, kind="ExternalInput").ap()
    qpat_d = nc.dram_tensor("qpat", [2, 768], BF16, kind="ExternalInput").ap()
    qindB_d = nc.dram_tensor("qindB", [2, 128], BF16, kind="ExternalInput").ap()
    oneh_d = nc.dram_tensor("oneh", [8, 8 * 128], BF16, kind="ExternalInput").ap()

    out_d = nc.dram_tensor("out", [NB, S, D], F32, kind="ExternalOutput").ap()

    qs_d = nc.dram_tensor("qs", [NB, D, S], BF16, kind=dt_scr).ap()
    ks_d = nc.dram_tensor("ks", [NB, D, S], BF16, kind=dt_scr).ap()
    vs_d = nc.dram_tensor("vs", [NB, S, D], BF16, kind=dt_scr).ap()
    cx_d = nc.dram_tensor("cx", [NB, 32, 2, 64, 8, 128], BF16, kind=dt_scr).ap()
    cs_d = nc.dram_tensor("cs", [NB, 32, H, 128], BF16, kind=dt_scr).ap()
    gbr_d = nc.dram_tensor("gbr", [NB, D], F32, kind="Internal").ap()
    dbg_d = nc.dram_tensor("dbg", [65, 64], F32, kind=dt_scr).ap() if dbg else None
    wts_d = nc.dram_tensor("wts", [NB * NT, 8, 128], F32, kind="Internal").ap()

    es = ExitStack()
    with es:
        p = Prog(nc, es)
        SB = lambda st, name, shp, dt: st.enter_context(nc.sbuf_tensor("s_" + name, shp, dt))
        PS = lambda st, name, shp, dt: st.enter_context(nc.psum_tensor("p_" + name, shp, dt))

        idb = SB(es, "idb", [128, 128], BF16)
        idf = SB(es, "idf", [128, 128], F32)
        ca1 = SB(es, "ca1", [128, 16], F32)
        sa_f = SB(es, "sa_f", [128, 16], F32)
        cb1 = SB(es, "cb1", [128, 16], F32)
        sb_f = SB(es, "sb_f", [128, 16], F32)
        gaE = SB(es, "gaE", [65, 32], F32)
        bqc = SB(es, "bqc", [128, 8], F32)
        bkc = SB(es, "bkc", [128, 8], F32)
        oneh = SB(es, "oneh", [8, 8 * 128], BF16)
        gw_sb = SB(es, "gw_sb", [128, 64], BF16)
        cst = SB(es, "cst", [128, 2], F32)
        nc.const_aps.register(F32, 1e-5, cst[:, 0:1])
        nc.const_aps.register(F32, -BCONST, cst[:, 1:2])

        sT = p.sem("sT")
        sV = p.sem("sV")
        sS = p.sem("sS")
        qW = p.sem("qW")
        qG = p.sem("qG")
        qM = p.sem("qM")
        qX = [p.sem(f"qX{i}") for i in range(16)]
        qQK = [p.sem(f"qQK{i}") for i in range(2)]
        qV_ = [p.sem(f"qV{i}") for i in range(2)]
        qVH = [p.sem(f"qVH{i}") for i in range(2)]
        gS1 = p.sem("gS1")
        gQ = [p.sem(f"gQ{i}") for i in range(2)]
        gK = [p.sem(f"gK{i}") for i in range(2)]
        gV2 = [p.sem(f"gV2{i}") for i in range(2)]
        gS2 = [p.sem(f"gS2{i}") for i in range(2)]
        gO = [p.sem(f"gO{i}") for i in range(2)]
        qEW = p.sem("qEW")
        qC = [p.sem(f"qC{i}") for i in range(2)]
        qPX = [p.sem(f"qPX{i}") for i in range(4)]
        sWF = p.sem("sWF")
        qWR = [p.sem(f"qWR{i}") for i in range(2)]
        qGB = p.sem("qGB")
        gX1 = [p.sem(f"gX1{i}") for i in range(2)]
        sOUT = p.sem("sOUT")

        p.new_block()
        es0 = ExitStack()
        c_sb = SB(es0, "c_sb", [NB, D], F32)
        scfm = SB(es0, "scfm", [128, 16], F32)
        wmb = [SB(es0, f"wmb{i}", [128, D], F32) for i in range(2)]
        en = SB(es0, "en", [NB, D], F32)
        sig = SB(es0, "sig", [NB, D], F32)
        silu = SB(es0, "silu", [NB, D], F32)
        modf = SB(es0, "modf", [128, 96], F32)
        gaf = SB(es0, "gaf", [128, 16], F32)
        gbf = SB(es0, "gbf", [128, 16], F32)
        garow = [SB(es0, f"garow{b}", [1, D], F32) for b in range(NB)]
        gbrow = [SB(es0, f"gbrow{b}", [1, D], F32) for b in range(NB)]
        b8 = SB(es0, "b8", [8, 128], F32)
        pm0 = PS(es0, "pm0", [128, 16], F32)
        pmm_ = [PS(es0, f"pmm{i}", [128, 2], F32) for i in range(2)]
        pm3 = PS(es0, "pm3", [1, D], F32)
        pm4 = PS(es0, "pm4", [128, 8], F32)
        pm5 = PS(es0, "pm5", [64, 32], F32)

        for src, dst in [(idb_d, idb), (idf_d, idf), (oneh_d, oneh),
                         (c_d, c_sb), (bq_d, b8)]:
            p.op("sync", lambda e, s=src, d=dst: e.dma_start(out=d[:, :], in_=s[:, :]),
                 inc=(qW, 16))
        p.op("gpsimd", lambda e: e.dma_start(out=gw_sb[:, :], in_=gw_d[:, :]),
             inc=(qG, 16))

        p.op("vector", lambda e: e.memset(cst[:, 0:1], 1e-5))
        p.op("vector", lambda e: e.memset(cst[:, 1:2], -BCONST), inc=(sV, 1))

        p.w("scalar", qW)
        p.op("scalar", lambda e: e.activation(en[:, :], c_sb[:, :], AF.Exp, scale=-1.0),
             inc=(sS, 1))
        p.w("vector", sS)
        p.op("vector", lambda e: e.tensor_scalar(sig[:, :], en[:, :], 1.0, None, ALU.add),
             inc=(sV, 1))
        p.w("vector", sV)
        p.op("vector", lambda e: e.reciprocal(sig[:, :], sig[:, :]), inc=(sV, 1))
        p.w("vector", sV)
        p.op("vector", lambda e: e.tensor_mul(silu[:, :], c_sb[:, :], sig[:, :]),
             inc=(sV, 1))
        p.w("tensor", sV)
        p.w("tensor", qW)
        for ic in range(8):
            p.op("tensor", lambda e, ic=ic: e.transpose(
                pm0[:, ic * 2:(ic + 1) * 2], silu[:, ic * 128:(ic + 1) * 128],
                idf[0:NB, 0:NB]), inc=(sT, 1))
        p.w("vector", sT)
        p.op("vector", lambda e: e.tensor_copy(scfm[:, :], pm0[:, :]), inc=(sV, 1))
        vc_scfm = sV.n

        tdone_oc = {}
        for oc in range(48):
            buf = wmb[oc % 2]
            pmm = pmm_[oc % 2]
            if oc >= 2:
                p.w("sync", sT, v=tdone_oc[oc - 2])
            p.op("sync", lambda e, oc=oc, b=buf: e.dma_start(out=b[:, :], in_=wm_d[oc]),
                 inc=(qM, 16))
            p.w("tensor", qM)
            p.w("tensor", sV, v=vc_scfm)
            if oc >= 2:
                p.w("tensor", sV)
            for ic in range(8):
                p.op("tensor", lambda e, b=buf, ic=ic, pmm=pmm: e.matmul(
                    pmm[:, :], b[:, ic * 128:(ic + 1) * 128],
                    scfm[:, ic * 2:(ic + 1) * 2],
                    start=(ic == 0), stop=(ic == 7)),
                    inc=(sT, 1) if ic == 7 else None)
            tdone_oc[oc] = sT.n
            p.w("vector", sT)
            p.op("vector", lambda e, oc=oc, pmm=pmm: e.tensor_copy(
                modf[:, oc * 2:(oc + 1) * 2], pmm[:, :]), inc=(sV, 1))

        p.w("vector", sV)
        p.op("vector", lambda e: e.tensor_scalar(ca1[:, :], modf[:, 16:32], 1.0, None, ALU.add))
        p.op("vector", lambda e: e.tensor_copy(sa_f[:, :], modf[:, 0:16]))
        p.op("vector", lambda e: e.tensor_scalar(cb1[:, :], modf[:, 64:80], 1.0, None, ALU.add))
        p.op("vector", lambda e: e.tensor_copy(sb_f[:, :], modf[:, 48:64]))
        gtmp = SB(es0, "gtmp", [128, 16], F32)
        for g_src, g_dst in [(32, gaf), (80, gbf)]:
            p.op("vector", lambda e, s=g_src, d=g_dst: e.tensor_scalar(
                d[:, :], modf[:, s:s + 16], 0.0, None, ALU.is_ge), inc=(sV, 1))
            p.w("vector", sV)
            p.op("vector", lambda e, d=g_dst: e.tensor_scalar(
                d[:, :], d[:, :], 2.0, -1.0, ALU.mult, ALU.add), inc=(sV, 1))
            p.op("vector", lambda e, s=g_src: e.tensor_scalar(
                gtmp[:, :], modf[:, s:s + 16], -1.0, None, ALU.mult), inc=(sV, 1))
            p.w("vector", sV)
            p.op("vector", lambda e, s=g_src: e.tensor_max(
                gtmp[:, :], gtmp[:, :], modf[:, s:s + 16]), inc=(sV, 1))
            p.w("vector", sV)
            p.op("vector", lambda e: e.tensor_scalar_max(
                gtmp[:, :], gtmp[:, :], 0.25), inc=(sV, 1))
            p.w("vector", sV)
            p.op("vector", lambda e, d=g_dst: e.tensor_mul(
                d[:, :], d[:, :], gtmp[:, :]), inc=(sV, 1))
        vc_mod = sV.n

        p.w("tensor", sV, v=vc_mod)
        for src, rows in [(gaf, garow), (gbf, gbrow)]:
            for b in range(NB):
                for ic in range(8):
                    p.op("tensor", lambda e, ic=ic, src=src, b=b: e.transpose(
                        pm3[:, ic * 128:(ic + 1) * 128],
                        src[:, ic * 2 + b: ic * 2 + b + 1], idf[:, :]),
                        inc=(sT, 1) if ic == 7 else None)
                p.w("vector", sT)
                p.op("vector", lambda e, rows=rows, b=b: e.tensor_copy(
                    rows[b][:, :], pm3[:, :]), inc=(sV, 1))
                p.w("tensor", sV)
        p.w("gpsimd", sV)
        for b in range(NB):
            p.op("gpsimd", lambda e, b=b: e.dma_start(
                out=gbr_d[b:b + 1, :], in_=gbrow[b][:, :]), inc=(qG, 16))
        for b in range(NB):
            for h in range(H):
                p.op("tensor", lambda e, b=b, h=h: e.transpose(
                    pm5[:, (b * H + h):(b * H + h) + 1],
                    garow[b][0:1, h * 64:(h + 1) * 64], idf[0:1, 0:1]),
                    inc=(sT, 1) if (b == NB - 1 and h == H - 1) else None)
        p.w("vector", sT)
        p.op("vector", lambda e: e.tensor_copy(gaE[0:64, :], pm5[:, :]))
        p.op("vector", lambda e: e.memset(gaE[64:65, :], 1.0), inc=(sV, 1))

        p.w("tensor", qW)
        p.op("tensor", lambda e: e.transpose(pm4[:, :], b8[:, :], idf[0:8, 0:8]),
             inc=(sT, 1))
        tq_b8 = sT.n
        p.w("vector", sT)
        p.op("vector", lambda e: e.tensor_copy(bqc[:, :], pm4[:, :]), inc=(sV, 1))
        p.w("sync", sT, v=tq_b8)
        p.op("sync", lambda e: e.dma_start(out=b8[:, :], in_=bk_d[:, :]), inc=(qW, 16))
        p.w("tensor", qW)
        p.w("tensor", sV)
        p.op("tensor", lambda e: e.transpose(pm4[:, :], b8[:, :], idf[0:8, 0:8]),
             inc=(sT, 1))
        p.w("vector", sT)
        p.op("vector", lambda e: e.tensor_copy(bkc[:, :], pm4[:, :]), inc=(sV, 1))

        if dbg:
            p.w("gpsimd", sV)
            p.op("gpsimd", lambda e: e.dma_start(out=dbg_d[:, 0:32], in_=gaE[:, :]),
                 inc=(qG, 16))
            p.op("gpsimd", lambda e: e.dma_start(
                out=dbg_d[0:1, 32:48],
                in_=bass.AP(garow[0], 0, [[64, 1], [1, 16]])), inc=(qG, 16))
            p.w("gpsimd", qG)
        p.flush()
        es0.close()

        p.new_block()
        es1 = ExitStack()
        wq_sb = SB(es1, "wq_sb", [128, 8 * D], BF16)
        wk_sb = SB(es1, "wk_sb", [128, 8 * D], BF16)
        wv_sb = SB(es1, "wv_sb", [128, 8 * D], BF16)
        bvr = SB(es1, "bvr", [128, D], F32)
        xbuf = [SB(es1, f"xbuf{i}", [128, D], F32) for i in range(16)]
        tb = [SB(es1, f"tb{i}", [128, D], BF16) for i in range(4)]
        junk = SB(es1, "junk", [128, D], BF16)
        stat = SB(es1, "stat", [128, 8], F32)
        hfm = [SB(es1, f"hfm{i}", [128, 8 * 512], BF16) for i in range(2)]
        qsb = [SB(es1, f"qsb{i}", [128, 512], BF16) for i in range(2)]
        ksb = [SB(es1, f"ksb{i}", [128, 512], BF16) for i in range(2)]
        vsb = [SB(es1, f"vsb{i}", [128, 512], BF16) for i in range(2)]
        pt1 = PS(es1, "pt1", [128, 1024], BF16)
        pq = [PS(es1, f"pq{i}", [128, 512], F32) for i in range(2)]
        pk = [PS(es1, f"pk{i}", [128, 512], F32) for i in range(2)]
        pv = [PS(es1, f"pv{i}", [128, 512], F32) for i in range(2)]

        for src, dst in [(wq_d, wq_sb), (wk_d, wk_sb), (wv_d, wv_sb)]:
            p.op("gpsimd", lambda e, s=src, d=dst: e.dma_start(out=d[:, :], in_=s[:, :]),
                 inc=(qG, 16))
        p.op("gpsimd", lambda e: e.dma_start(
            out=bvr[:, :], in_=bass.AP(bv_d.tensor, 0, [[0, 128], [1, D]])),
            inc=(qG, 16))
        qg_w = qG.n

        x_cons = [0] * 16
        t_cons = [0, 0, 0, 0]
        pq_ev = [0, 0]
        pk_ev = [0, 0]
        pv_ev = [0, 0]
        ptr_ev = [0]
        last_var = [0]
        hfm_cons = [0, 0]
        hfm_done = [0, 0]
        NSUP = NB * NST

        def p1_tilework(s_, taus=range(4)):
            b, st_ = divmod(s_, NST)
            t0 = st_ * 512
            hf = hfm[s_ % 2]
            for tau in taus:
                bi = (s_ * 4 + tau) % 16
                ti = (s_ * 4 + tau) % 4
                tt0 = t0 + tau * 128
                p.w("sync", sV, v=x_cons[bi])
                p.op("sync", lambda e, b=b, tt0=tt0, bi=bi: e.dma_start(
                    out=xbuf[bi][:, :], in_=x_d[b, tt0:tt0 + 128, :]),
                    inc=(qX[bi], 16))
                p.w("vector", qX[bi])
                p.op("vector", lambda e, bi=bi: e.reduce_sum(
                    stat[:, 0:1], xbuf[bi][:, :], axis=AX), inc=(sV, 1))
                p.w("scalar", qX[bi])
                p.w("scalar", sV, v=last_var[0])
                p.op("scalar", lambda e, bi=bi: e.activation(
                    junk[:, :], xbuf[bi][:, :], AF.Square,
                    accum_out=stat[:, 1:2]), inc=(sS, 1))
                p.w("vector", sV)
                p.w("vector", sS)
                p.op("vector", lambda e: e.tensor_scalar(
                    stat[:, 2:3], stat[:, 0:1], 1.0 / D, None, ALU.mult), inc=(sV, 1))
                p.w("vector", sV)
                p.op("vector", lambda e: e.tensor_mul(
                    stat[:, 3:4], stat[:, 2:3], stat[:, 2:3]), inc=(sV, 1))
                p.w("vector", sV)
                p.op("vector", lambda e: e.scalar_tensor_tensor(
                    stat[:, 4:5], stat[:, 1:2], 1.0 / D, stat[:, 3:4],
                    ALU.mult, ALU.subtract), inc=(sV, 1))
                last_var[0] = sV.n
                p.w("scalar", sV)
                p.op("scalar", lambda e: e.activation(
                    stat[:, 5:6], stat[:, 4:5], AF.Ln, bias=1e-5), inc=(sS, 1))
                p.w("scalar", sS)
                p.op("scalar", lambda e: e.activation(
                    stat[:, 6:7], stat[:, 5:6], AF.Exp, scale=-0.5), inc=(sS, 1))
                p.w("vector", sS)
                p.w("vector", sT, v=t_cons[ti])
                p.op("vector", lambda e, bi=bi, ti=ti: e.tensor_scalar(
                    tb[ti][:, :], xbuf[bi][:, :], stat[:, 2:3], stat[:, 6:7],
                    ALU.subtract, ALU.mult), inc=(sV, 1))
                x_cons[bi] = sV.n
                vt = sV.n
                p.w("tensor", sV, v=vt)
                p.w("tensor", sV, v=ptr_ev[0])
                for ic in range(8):
                    p.op("tensor", lambda e, ti=ti, ic=ic: e.transpose(
                        pt1[:, ic * 128:(ic + 1) * 128],
                        tb[ti][:, ic * 128:(ic + 1) * 128], idb[:, :]),
                        inc=(sT, 1) if ic == 7 else None)
                t_cons[ti] = sT.n
                p.w("vector", sT)
                p.w("vector", sT, v=hfm_cons[s_ % 2])
                for ic in range(8):
                    p.op("vector", lambda e, ic=ic, tau=tau, b=b, hf=hf: e.tensor_scalar(
                        hf[:, ic * 512 + tau * 128: ic * 512 + (tau + 1) * 128],
                        pt1[:, ic * 128:(ic + 1) * 128],
                        ca1[:, ic * 2 + b: ic * 2 + b + 1],
                        sa_f[:, ic * 2 + b: ic * 2 + b + 1],
                        ALU.mult, ALU.add), inc=(sV, 1) if ic == 7 else None)
                ptr_ev[0] = sV.n
            hfm_done[s_ % 2] = sV.n

        def p1_mms(s_, part):
            b, st_ = divmod(s_, NST)
            t0 = st_ * 512
            hf = hfm[s_ % 2]
            vh = hfm_done[s_ % 2]
            for wsb, bias, psm, pev, osb, gsem, scratch in (
                    ((wq_sb, bqc, pq, pq_ev, qsb, gQ, qs_d),
                     (wk_sb, bkc, pk, pk_ev, ksb, gK, ks_d))[part:part + 1]):
                for oc in range(8):
                    pi = oc % 2
                    pm = psm[pi]
                    p.w("tensor", sS, v=pev[pi])
                    p.w("tensor", sV, v=vh)
                    if s_ == 0 and oc == 0:
                        p.w("tensor", qG, v=qg_w)
                    for ic in range(8):
                        p.op("tensor", lambda e, ic=ic, oc=oc, wsb=wsb, pm=pm, hf=hf: e.matmul(
                            pm[:, :],
                            wsb[:, ic * D + oc * 128: ic * D + (oc + 1) * 128],
                            hf[:, ic * 512:(ic + 1) * 512],
                            start=(ic == 0), stop=(ic == 7)),
                            inc=(sT, 1) if ic == 7 else None)
                    p.w("scalar", sT, v=sT.n)
                    p.w("scalar", gsem[pi], v=gsem[pi].n)
                    p.op("scalar", lambda e, pm=pm, pi=pi, oc=oc, osb=osb, bias=bias: e.activation(
                        osb[pi][:, :], pm[:, :], AF.Identity,
                        bias=bias[:, oc:oc + 1]), inc=(sS, 1))
                    pev[pi] = sS.n
                    p.w("gpsimd", sS, v=sS.n)
                    p.op("gpsimd", lambda e, b=b, oc=oc, t0=t0, pi=pi, osb=osb, scratch=scratch: e.dma_start(
                        out=scratch[b, oc * 128:(oc + 1) * 128, t0:t0 + 512],
                        in_=osb[pi][:, :]), inc=(gsem[pi], 16))
            if part != 2:
                return
            for tau in range(4):
                for oh in range(2):
                    pi = (tau * 2 + oh) % 2
                    pm = pv[pi]
                    p.w("tensor", sV, v=pv_ev[pi])
                    for ic in range(8):
                        p.op("tensor", lambda e, ic=ic, tau=tau, oh=oh, pm=pm, hf=hf: e.matmul(
                            pm[:, :],
                            hf[:, ic * 512 + tau * 128: ic * 512 + (tau + 1) * 128],
                            wv_sb[:, ic * D + oh * 512: ic * D + (oh + 1) * 512],
                            start=(ic == 0), stop=(ic == 7)),
                            inc=(sT, 1) if ic == 7 else None)
                    p.w("vector", sT, v=sT.n)
                    p.w("vector", gV2[pi], v=gV2[pi].n)
                    p.op("vector", lambda e, pm=pm, pi=pi, oh=oh: e.scalar_tensor_tensor(
                        vsb[pi][:, :], pm[:, :], 1.0,
                        bvr[:, oh * 512:(oh + 1) * 512], ALU.mult, ALU.add),
                        inc=(sV, 1))
                    pv_ev[pi] = sV.n
                    p.w("gpsimd", sV, v=sV.n)
                    p.op("gpsimd", lambda e, b=b, t0=t0, tau=tau, oh=oh, pi=pi: e.dma_start(
                        out=vs_d[b, t0 + tau * 128: t0 + (tau + 1) * 128,
                                 oh * 512:(oh + 1) * 512],
                        in_=vsb[pi][:, :]), inc=(gV2[pi], 16))
            hfm_cons[s_ % 2] = sT.n

        p1_tilework(0)
        for s_ in range(NSUP):
            p1_mms(s_, 2)
            if s_ + 1 < NSUP:
                p1_tilework(s_ + 1, (0, 1))
            p1_mms(s_, 0)
            if s_ + 1 < NSUP:
                p1_tilework(s_ + 1, (2, 3))
            p1_mms(s_, 1)

        for sem in (gQ[0], gQ[1], gK[0], gK[1], gV2[0], gV2[1]):
            p.w("gpsimd", sem)
        p.flush()
        es1.close()

        ew_sb = SB(es, "ew_sb", [128, 8 * 8 * D], BF16)

        p.new_block()
        es2 = ExitStack()
        qb2 = [SB(es2, f"qb2{i}", [128, S], BF16) for i in range(2)]
        kb2 = [SB(es2, f"kb2{i}", [128, S], BF16) for i in range(2)]
        vbL = [SB(es2, f"vbL{i}", [128, 32 * 65], BF16) for i in range(2)]
        vbH = [SB(es2, f"vbH{i}", [128, 33 * 65], BF16) for i in range(2)]
        expA = [SB(es2, f"expA{i}", [128, 1024], BF16) for i in range(2)]
        expM = [SB(es2, f"expM{i}", [128, 1024], BF16) for i in range(3)]
        cxs = [SB(es2, f"cxs{i}", [65, S], BF16) for i in range(2)]
        pSS = [PS(es2, f"pSS{i}", [128, 1024], F32) for i in range(3)]
        pC = [PS(es2, f"pC{i}", [65, 512], F32) for i in range(2)]

        for e_ in range(8):
            p.op("gpsimd", lambda e, e_=e_: e.dma_start(
                out=ew_sb[:, e_ * 8 * D:(e_ + 1) * 8 * D], in_=ew_d[e_]),
                inc=(qEW, 16))

        for i in range(2):
            p.op("vector", lambda e, i=i: e.memset(
                bass.AP(vbL[i], 64, [[32 * 65, 128], [65, 32], [1, 1]]), 1.0))
            p.op("vector", lambda e, i=i: e.memset(
                bass.AP(vbH[i], 64, [[33 * 65, 128], [65, 33], [1, 1]]), 1.0),
                inc=(sV, 1))
        vones = sV.n

        qk_cons = [0, 0]
        vbL_cons = [0, 0]
        vbH_cons = [0, 0]
        cx_st = [0, 0]
        expA_cons = [0, 0]
        pA_ev = [0, 0]
        pS3_ev = [[0, 0], [0, 0], [0, 0]]
        em3_cons = [[0, 0], [0, 0], [0, 0]]
        emB_cons = [0]
        pC_ev = [[0, 0], [0, 0]]

        for b in range(NB):
            for pp in range(H // 2):
                pi = pp % 2
                cols = [b * H + 2 * pp, b * H + 2 * pp + 1]
                p.w("sync", sT, v=qk_cons[pi])
                for h01 in range(2):
                    h = 2 * pp + h01
                    p.op("sync", lambda e, b=b, h=h, pi=pi, h01=h01: e.dma_start(
                        out=bass.AP(qb2[pi], h01 * 64 * S, [[S, 64], [1, S]]),
                        in_=qs_d[b, h * 64:(h + 1) * 64, :]), inc=(qQK[pi], 16))
                    p.op("sync", lambda e, b=b, h=h, pi=pi, h01=h01: e.dma_start(
                        out=bass.AP(kb2[pi], h01 * 64 * S, [[S, 64], [1, S]]),
                        in_=ks_d[b, h * 64:(h + 1) * 64, :]), inc=(qQK[pi], 16))
                for h01 in range(2):
                    h = 2 * pp + h01
                    p.w("sync", sT, v=vbL_cons[h01])
                    p.op("sync", lambda e, b=b, h=h, h01=h01: e.dma_start(
                        out=bass.AP(vbL[h01], 0, [[32 * 65, 128], [65, 32], [1, 64]]),
                        in_=bass.AP(vs_d.tensor, b * S * D + h * 64,
                                    [[D, 128], [128 * D, 32], [1, 64]])),
                        inc=(qV_[h01], 16))
                for h01 in range(2):
                    h = 2 * pp + h01
                    p.w("sync", sT, v=vbH_cons[h01])
                    p.op("sync", lambda e, b=b, h=h, h01=h01: e.dma_start(
                        out=bass.AP(vbH[h01], 0, [[33 * 65, 128], [65, 31], [1, 64]]),
                        in_=bass.AP(vs_d.tensor, b * S * D + 64 * D + h * 64,
                                    [[D, 128], [128 * D, 31], [1, 64]])),
                        inc=(qVH[h01], 16))
                    for r, (t0f, t0b) in enumerate([(0, 128), (S - 64, S - 192)]):
                        p.op("sync", lambda e, b=b, h=h, h01=h01, r=r, t0f=t0f: e.dma_start(
                            out=bass.AP(vbH[h01], 31 * 65 + r * 64 * 33 * 65,
                                        [[33 * 65, 64], [1, 64]]),
                            in_=bass.AP(vs_d.tensor, b * S * D + t0f * D + h * 64,
                                        [[D, 64], [1, 64]])),
                            inc=(qVH[h01], 16))
                        p.op("sync", lambda e, b=b, h=h, h01=h01, r=r, t0b=t0b: e.dma_start(
                            out=bass.AP(vbH[h01], 32 * 65 + r * 64 * 33 * 65,
                                        [[33 * 65, 64], [1, 64]]),
                            in_=bass.AP(vs_d.tensor, b * S * D + t0b * D + h * 64,
                                        [[D, 64], [1, 64]])),
                            inc=(qVH[h01], 16))

                p.w("tensor", qQK[pi])
                p.w("tensor", qV_[0])
                p.w("tensor", qV_[1])
                if b == 0 and pp == 0:
                    p.w("tensor", sV, v=vones)

                qA = [bass.AP(qb2[pi], h01 * 64 * S, [[S, 64], [4032, 2], [1, 64]])
                      for h01 in range(2)]
                qB_ = [bass.AP(qb2[pi], h01 * 64 * S + 64, [[S, 64], [3904, 2], [1, 64]])
                       for h01 in range(2)]
                ksl = [lambda c0, c1, h01=h01: bass.AP(
                    kb2[pi], h01 * 64 * S + c0, [[S, 64], [1, c1 - c0]])
                    for h01 in range(2)]

                for g in range(9):
                    if g < 8:
                        bank = pSS[g % 2]
                        p.w("tensor", sS, v=pA_ev[g % 2])
                        if g < 2:
                            p.w("tensor", sS, v=pS3_ev[g % 2][0])
                            p.w("tensor", sS, v=pS3_ev[g % 2][1])
                        for j in range(4):
                            c = g * 4 + j
                            for h01 in range(2):
                                p.op("tensor", lambda e, c=c, j=j, bank=bank, pi=pi,
                                     h01=h01, qA=qA: e.matmul(
                                    bank[:, h01 * 512 + j * 128: h01 * 512 + (j + 1) * 128],
                                    bass.AP(kb2[pi], h01 * 64 * S + c * 128,
                                            [[S, 64], [1, 128]]),
                                    qA[h01], start=True, stop=True),
                                    inc=(sT, 1) if (j == 3 and h01 == 1) else None)
                        tdata = sT.n
                        p.w("scalar", sT, v=tdata)
                        p.w("scalar", sT, v=expA_cons[g % 2])
                        p.op("scalar", lambda e, g=g, bank=bank: e.activation(
                            expA[g % 2][:, :], bank[:, :],
                            AF.Exp, scale=0.125, bias=-BCONST), inc=(sS, 1))
                        pA_ev[g % 2] = sS.n
                    if g > 0:
                        gg = g - 1
                        p.w("tensor", sS, v=pA_ev[gg % 2])
                        for h01 in range(2):
                            for j in range(4):
                                c = gg * 4 + j
                                p.op("tensor", lambda e, c=c, gg=gg, j=j, h01=h01: e.matmul(
                                    pC[h01][:, 0:128],
                                    vbL[h01][:, c * 65: c * 65 + 65],
                                    expA[gg % 2][:, h01 * 512 + j * 128: h01 * 512 + (j + 1) * 128],
                                    start=(c == 0), stop=(c == 31)),
                                    inc=(sT, 1) if (j == 3 and h01 == 1) else None)
                        expA_cons[gg % 2] = sT.n
                p.w("vector", sT)
                for h01 in range(2):
                    p.w("vector", gS2[h01], v=cx_st[h01])
                    p.op("vector", lambda e, h01=h01, col=cols[h01]: e.tensor_scalar(
                        bass.AP(cxs[h01], 0, [[S, 65], [4032, 2], [1, 64]]),
                        pC[h01][:, 0:128], gaE[:, col:col + 1], None, ALU.mult),
                        inc=(sV, 1))
                    pC_ev[h01][0] = sV.n

                bank = pSS[0]
                p.w("tensor", sS, v=pA_ev[0])
                for c in range(3):
                    for h01 in range(2):
                        if c == 1:
                            for r, c0 in enumerate((128, 3904)):
                                p.op("tensor", lambda e, c=c, h01=h01, r=r, c0=c0,
                                     bank=bank, qB_=qB_, pi=pi: e.matmul(
                                    bank[r * 64:(r + 1) * 64,
                                         h01 * 512 + c * 128: h01 * 512 + (c + 1) * 128],
                                    bass.AP(kb2[pi], h01 * 64 * S + c0,
                                            [[S, 64], [1, 64]]),
                                    qB_[h01], start=True, stop=True))
                            continue
                        c0 = 0 if c == 0 else 3968
                        kc = bass.AP(kb2[pi], h01 * 64 * S + c0, [[S, 64], [1, 128]])
                        p.op("tensor", lambda e, c=c, h01=h01, bank=bank, kc=kc, qB_=qB_: e.matmul(
                            bank[:, h01 * 512 + c * 128: h01 * 512 + (c + 1) * 128],
                            kc, qB_[h01], start=True, stop=True),
                            inc=(sT, 1) if (c == 2 and h01 == 1) else None)
                p.w("scalar", sT)
                p.w("scalar", sT, v=emB_cons[0])
                p.op("scalar", lambda e, bank=bank: e.activation(
                    expA[0][:, :], bank[:, :], AF.Exp, scale=0.125, bias=-BCONST),
                    inc=(sS, 1))
                pA_ev[0] = sS.n
                pS3_ev[0][0] = sS.n
                pS3_ev[0][1] = sS.n
                p.w("vector", sS)
                for h01 in range(2):
                    o = h01 * 512
                    p.op("vector", lambda e, o=o: e.memset(expA[0][64:128, o + 64:o + 128], 0.0))
                    p.op("vector", lambda e, o=o: e.memset(expA[0][0:64, o + 192:o + 256], 0.0))
                    p.op("vector", lambda e, o=o: e.memset(expA[0][64:128, o + 128:o + 192], 0.0))
                    p.op("vector", lambda e, o=o: e.memset(expA[0][0:64, o + 256:o + 320], 0.0),
                         inc=(sV, 1) if h01 == 1 else None)
                vmB = sV.n
                p.w("tensor", sV, v=vmB)
                p.w("tensor", sS)
                p.w("tensor", qVH[0])
                p.w("tensor", qVH[1])
                for h01 in range(2):
                    vchB = [(vbL, 0), (vbH, 32 * 65), (vbL, 31 * 65)]
                    for c in range(3):
                        vbuf, vco = vchB[c]
                        p.op("tensor", lambda e, c=c, h01=h01, vbuf=vbuf, vco=vco: e.matmul(
                            pC[h01][:, 0:128], vbuf[h01][:, vco:vco + 65],
                            expA[0][:, h01 * 512 + c * 128: h01 * 512 + (c + 1) * 128],
                            start=(c == 0), stop=(c == 2)),
                            inc=(sT, 1) if (c == 2 and h01 == 1) else None)
                emB_cons[0] = sT.n
                expA_cons[0] = sT.n
                vbL_cons[0] = sT.n
                vbL_cons[1] = sT.n
                p.w("vector", sT)
                for h01 in range(2):
                    p.op("vector", lambda e, h01=h01, col=cols[h01]: e.tensor_scalar(
                        bass.AP(cxs[h01], 64, [[S, 65], [3904, 2], [1, 64]]),
                        pC[h01][:, 0:128], gaE[:, col:col + 1], None, ALU.mult),
                        inc=(sV, 1))
                    pC_ev[h01][0] = sV.n

                vm = {}
                for g in range(16):
                    if g > 0:
                        gg = g - 1
                        for h01 in range(2):
                            bi3 = (2 * gg + h01) % 3
                            em = expM[bi3]
                            pcb = pC[h01]
                            reg = (gg % 2) * 256
                            p.w("tensor", sV, v=pC_ev[h01][gg % 2])
                            vch = [31 * 65, (2 * gg) * 65, (2 * gg + 1) * 65,
                                   (2 * gg + 2) * 65]
                            p.w("tensor", sV, v=vm[(gg, h01, 0)])
                            for c in range(2):
                                p.op("tensor", lambda e, c=c, pcb=pcb, reg=reg, h01=h01,
                                     vch=vch, em=em: e.matmul(
                                    pcb[:, reg:reg + 256], vbH[h01][:, vch[c]:vch[c] + 65],
                                    em[:, c * 256:(c + 1) * 256],
                                    start=(c == 0), stop=False))
                            p.w("tensor", sV, v=vm[(gg, h01, 1)])
                            for c in range(2, 4):
                                p.op("tensor", lambda e, c=c, pcb=pcb, reg=reg, h01=h01,
                                     vch=vch, em=em: e.matmul(
                                    pcb[:, reg:reg + 256], vbH[h01][:, vch[c]:vch[c] + 65],
                                    em[:, c * 256:(c + 1) * 256],
                                    start=False, stop=(c == 3)),
                                    inc=(sT, 1) if c == 3 else None)
                            em3_cons[bi3][0] = sT.n
                            em3_cons[bi3][1] = sT.n
                            p.w("vector", sT)
                            p.op("vector", lambda e, gg=gg, h01=h01, col=cols[h01],
                                 pcb=pcb, reg=reg: e.tensor_scalar(
                                cxs[h01][:, (4 * gg + 2) * 64:(4 * gg + 2) * 64 + 256],
                                pcb[:, reg:reg + 256], gaE[:, col:col + 1], None, ALU.mult),
                                inc=(sV, 1))
                            pC_ev[h01][gg % 2] = sV.n
                    if g < 15:
                        q0 = (4 * g + 2) * 64
                        for h01 in range(2):
                            bi3 = (2 * g + h01) % 3
                            bk = pSS[bi3]
                            em = expM[bi3]
                            qsp = bass.AP(qb2[pi], h01 * 64 * S + q0, [[S, 64], [1, 256]])
                            p.w("tensor", sS, v=pS3_ev[bi3][0])
                            p.w("tensor", sS, v=pS3_ev[bi3][1])
                            for r, c0 in enumerate((0, 4032)):
                                p.op("tensor", lambda e, r=r, c0=c0, bk=bk, qsp=qsp,
                                     pi=pi, h01=h01: e.matmul(
                                    bk[r * 64:(r + 1) * 64, 0:256],
                                    bass.AP(kb2[pi], h01 * 64 * S + c0,
                                            [[S, 64], [1, 64]]),
                                    qsp, start=True, stop=True))
                            p.op("tensor", lambda e, g=g, bk=bk, qsp=qsp, pi=pi, h01=h01: e.matmul(
                                bk[:, 256:512],
                                bass.AP(kb2[pi], h01 * 64 * S + (4 * g + 1) * 64,
                                        [[S, 64], [1, 128]]),
                                qsp, start=True, stop=True), inc=(sT, 1))
                            t0g = sT.n
                            p.op("tensor", lambda e, g=g, bk=bk, qsp=qsp, pi=pi, h01=h01: e.matmul(
                                bk[:, 512:768],
                                bass.AP(kb2[pi], h01 * 64 * S + (4 * g + 3) * 64,
                                        [[S, 64], [1, 128]]),
                                qsp, start=True, stop=True))
                            p.op("tensor", lambda e, g=g, bk=bk, qsp=qsp, pi=pi, h01=h01: e.matmul(
                                bk[:, 768:1024],
                                bass.AP(kb2[pi], h01 * 64 * S + (4 * g + 5) * 64,
                                        [[S, 64], [1, 128]]),
                                qsp, start=True, stop=True), inc=(sT, 1))
                            t1g = sT.n
                            p.w("scalar", sT, v=t0g)
                            p.w("scalar", sT, v=em3_cons[bi3][0])
                            p.op("scalar", lambda e, bk=bk, em=em: e.activation(
                                em[:, 0:512], bk[:, 0:512], AF.Exp, scale=0.125,
                                bias=-BCONST), inc=(sS, 1))
                            pS3_ev[bi3][0] = sS.n
                            p.w("vector", sS)
                            p.op("vector", lambda e, em=em: e.memset(em[0:64, 256 + 64:512], 0.0))
                            p.op("vector", lambda e, em=em: e.memset(em[64:128, 256 + 128:512], 0.0),
                                 inc=(sV, 1))
                            vm[(g, h01, 0)] = sV.n
                            p.w("scalar", sT, v=t1g)
                            p.w("scalar", sT, v=em3_cons[bi3][1])
                            p.op("scalar", lambda e, bk=bk, em=em: e.activation(
                                em[:, 512:1024], bk[:, 512:1024], AF.Exp, scale=0.125,
                                bias=-BCONST), inc=(sS, 1))
                            pS3_ev[bi3][1] = sS.n
                            p.w("vector", sS)
                            p.op("vector", lambda e, em=em: e.memset(em[0:64, 512 + 192:768], 0.0))
                            p.op("vector", lambda e, em=em: e.memset(em[64:128, 512:512 + 64], 0.0))
                            p.op("vector", lambda e, em=em: e.memset(em[0:64, 768:768 + 128], 0.0))
                            p.op("vector", lambda e, em=em: e.memset(em[64:128, 768:768 + 192], 0.0),
                                 inc=(sV, 1))
                            vm[(g, h01, 1)] = sV.n
                qk_cons[pi] = sT.n
                vbH_cons[0] = sT.n
                vbH_cons[1] = sT.n
                for h01 in range(2):
                    h = 2 * pp + h01
                    p.w("gpsimd", sV, v=pC_ev[h01][0])
                    hp_, pair_ = h % 2, h // 2
                    for dh in range(2):
                        p.op("gpsimd", lambda e, b=b, hp_=hp_, pair_=pair_, dh=dh, h01=h01: e.dma_start(
                            out=bass.AP(cx_d.tensor,
                                        b * 32 * 2 * 64 * 8 * 128 + hp_ * 64 * 8 * 128
                                        + dh * 32 * 8 * 128 + pair_ * 128,
                                        [[8 * 128, 32], [2 * 64 * 8 * 128, 32], [1, 128]]),
                            in_=bass.AP(cxs[h01], dh * 32 * S, [[S, 32], [128, 32], [1, 128]])),
                            inc=(gS2[h01], 16))
                    p.op("gpsimd", lambda e, b=b, h=h, h01=h01: e.dma_start(
                        out=bass.AP(cs_d.tensor, b * 32 * H * 128 + h * 128,
                                    [[32 * H * 128, 1], [H * 128, 32], [1, 128]]),
                        in_=bass.AP(cxs[h01], 64 * S, [[S, 1], [128, 32], [1, 128]])),
                        inc=(gS2[h01], 16))
                    cx_st[h01] = gS2[h01].n

        p.w("gpsimd", gS2[0])
        p.w("gpsimd", gS2[1])
        p.flush()
        es2.close()

        p.new_block()
        es3 = ExitStack()
        gbrep = SB(es3, "gbrep", [128, NB * D], BF16)
        px = [SB(es3, f"px{i}", [128, D], F32) for i in range(4)]
        cxp = [SB(es3, f"cxp{i}", [128, 8 * 128], BF16) for i in range(2)]
        sums = [SB(es3, f"sums{i}", [16, 128], BF16) for i in range(2)]
        sumf = SB(es3, "sumf", [128, 16], F32)
        rcp = [SB(es3, f"rcp{i}", [128, 16], F32) for i in range(2)]
        x1 = [SB(es3, f"x1{i}", [128, D], F32) for i in range(4)]
        tmpb2 = [SB(es3, f"tmpb2{i}", [128, 512], F32) for i in range(2)]
        tb2 = [SB(es3, f"tb2{i}", [128, D], BF16) for i in range(2)]
        stat2 = SB(es3, "stat2", [128, 8], F32)
        h2fm = [SB(es3, f"h2fm{i}", [128, 8 * 128], BF16) for i in range(2)]
        scl = [SB(es3, f"scl{i}", [128, 8 * 128], BF16) for i in range(8)]
        wrep = [SB(es3, f"wrep{i}", [128, 8 * 128], BF16) for i in range(2)]
        top8 = SB(es3, "top8", [128, 8], F32)
        m1 = SB(es3, "m1", [128, 8], F32)
        m2 = SB(es3, "m2", [128, 8], F32)
        wcols = SB(es3, "wcols", [128, 4], F32)
        wf = SB(es3, "wf", [128, 8], F32)
        wft = SB(es3, "wft", [8, 128], BF16)
        pTc = PS(es3, "pTc", [128, 1024], BF16)
        pE = [PS(es3, f"pE{i}", [128, 512], F32) for i in range(4)]
        pG = PS(es3, "pG", [128, 8], F32)
        pSm = PS(es3, "pSm", [128, 16], BF16)
        pWf = PS(es3, "pWf", [8, 128], F32)

        for b in range(NB):
            p.op("gpsimd", lambda e, b=b: e.dma_start(
                out=gbrep[:, b * D:(b + 1) * D],
                in_=bass.AP(gbr_d.tensor, b * D, [[0, 128], [1, D]])), inc=(qGB, 16))

        px_cons = [0, 0, 0, 0]
        x1_st = [0, 0]
        tmpb_st = [0, 0]
        cxp_cons = [0, 0]
        sums_cons = [0, 0]
        h2_cons = [0, 0]
        scl_cons = [0] * 8
        wrep_cons = [0, 0]
        scl_done = {}
        pTc_ev = [0]
        pE_ev = [0, 0]
        pG_ev = [0]
        pSm_ev = [0]
        pWf_ev = [0]
        h2t_done = [0, 0]
        wrep_rdy = [0, 0]
        texp_by = {}
        last_var2 = [0]

        def p3_pre_loads(i):
            b, tau = i // NT, i % NT
            t0 = tau * 128
            pi = i % 2
            px4 = i % 4
            p.w("sync", sV, v=px_cons[px4])
            p.op("sync", lambda e, b=b, t0=t0, px4=px4: e.dma_start(
                out=px[px4][:, :], in_=x_d[b, t0:t0 + 128, :]), inc=(qPX[px4], 16))
            p.w("sync", sT, v=max(cxp_cons[pi], sums_cons[pi]))
            p.op("sync", lambda e, b=b, tau=tau, pi=pi: e.dma_start(
                out=cxp[pi][:, :], in_=bass.AP(cx_d.tensor,
                    (b * 32 + tau) * 2 * 64 * 8 * 128, [[1024, 128], [1, 1024]])),
                inc=(qC[pi], 16))
            p.op("sync", lambda e, b=b, tau=tau, pi=pi: e.dma_start(
                out=sums[pi][:, :], in_=bass.AP(cs_d.tensor,
                    (b * 32 + tau) * H * 128, [[128, 16], [1, 128]])),
                inc=(qC[pi], 16))
            if i == 0:
                p.w("vector", qGB)

        def p3_pre_ctxT(i):
            pi = i % 2
            p.w("tensor", qC[pi])
            p.w("tensor", sV, v=pTc_ev[0])
            for r in range(8):
                p.op("tensor", lambda e, r=r, pi=pi: e.transpose(
                    pTc[:, r * 128:(r + 1) * 128], cxp[pi][:, r * 128:(r + 1) * 128],
                    idb[:, :]), inc=(sT, 1) if r == 7 else None)
            cxp_cons[pi] = sT.n
            p.w("tensor", sV, v=pSm_ev[0])
            p.op("tensor", lambda e, pi=pi: e.transpose(
                pSm[:, :], sums[pi][:, :], idb[0:16, 0:16]), inc=(sT, 1))
            sums_cons[pi] = sT.n
            p.w("vector", sT)
            p.op("vector", lambda e: e.tensor_copy(sumf[:, :], pSm[:, :]), inc=(sV, 1))
            pSm_ev[0] = sV.n
            p.w("vector", sV)
            p.op("vector", lambda e, pi=pi: e.reciprocal(rcp[pi][:, :], sumf[:, :]),
                 inc=(sV, 1))

        def p3_pre_x1(i):
            pi = i % 2
            x14 = i % 4
            p.w("vector", sV)
            p.w("vector", qPX[x14])
            for hh in range(H):
                p.op("vector", lambda e, hh=hh, pi=pi, x14=x14: e.scalar_tensor_tensor(
                    x1[x14][:, hh * 64:(hh + 1) * 64], pTc[:, hh * 64:(hh + 1) * 64],
                    rcp[pi][:, hh:hh + 1], px[x14][:, hh * 64:(hh + 1) * 64],
                    ALU.mult, ALU.add), inc=(sV, 1) if hh == H - 1 else None)
            px_cons[x14] = sV.n
            pTc_ev[0] = sV.n
            b_, tau_ = i // NT, i % NT
            p.w("gpsimd", sV, v=sV.n)
            p.op("gpsimd", lambda e, b_=b_, tau_=tau_, x14=x14: e.dma_start(
                out=out_d[b_, tau_ * 128:tau_ * 128 + 128, :], in_=x1[x14][:, :]),
                inc=(gX1[i % 2], 16))
            x1_st[i % 2] = gX1[i % 2].n
            p.op("vector", lambda e, x14=x14: e.reduce_sum(
                stat2[:, 0:1], x1[x14][:, :], axis=AX), inc=(sV, 1))
            p.w("scalar", sV)
            p.w("scalar", sV, v=last_var2[0])
            p.w("scalar", sT, v=h2t_done[pi])
            p.op("scalar", lambda e, x14=x14, pi=pi: e.activation(
                tb2[pi][:, :], x1[x14][:, :], AF.Square, accum_out=stat2[:, 1:2]),
                inc=(sS, 1))
            p.w("vector", sV)
            p.w("vector", sS)
            p.op("vector", lambda e: e.tensor_scalar(
                stat2[:, 2:3], stat2[:, 0:1], 1.0 / D, None, ALU.mult), inc=(sV, 1))
            p.w("vector", sV)
            p.op("vector", lambda e: e.tensor_mul(
                stat2[:, 3:4], stat2[:, 2:3], stat2[:, 2:3]), inc=(sV, 1))
            p.w("vector", sV)
            p.op("vector", lambda e: e.scalar_tensor_tensor(
                stat2[:, 4:5], stat2[:, 1:2], 1.0 / D, stat2[:, 3:4],
                ALU.mult, ALU.subtract), inc=(sV, 1))
            last_var2[0] = sV.n
            p.w("scalar", sV)
            p.op("scalar", lambda e: e.activation(
                stat2[:, 5:6], stat2[:, 4:5], AF.Ln, bias=1e-5), inc=(sS, 1))
            p.w("scalar", sS)
            p.op("scalar", lambda e: e.activation(
                stat2[:, 6:7], stat2[:, 5:6], AF.Exp, scale=-0.5), inc=(sS, 1))
            p.w("vector", sS)
            p.op("vector", lambda e, pi=pi, x14=x14: e.tensor_scalar(
                tb2[pi][:, :], x1[x14][:, :], stat2[:, 2:3], stat2[:, 6:7],
                ALU.subtract, ALU.mult), inc=(sV, 1))

        def p3_pre_h2T(i):
            b = i // NT
            pi = i % 2
            p.w("tensor", sV)
            p.w("tensor", sV, v=pTc_ev[0])
            for ic in range(8):
                p.op("tensor", lambda e, ic=ic, pi=pi: e.transpose(
                    pTc[:, ic * 128:(ic + 1) * 128], tb2[pi][:, ic * 128:(ic + 1) * 128],
                    idb[:, :]), inc=(sT, 1) if ic == 7 else None)
            h2t_done[pi] = sT.n
            p.w("vector", sT)
            p.w("vector", sT, v=h2_cons[pi])
            for ic in range(8):
                p.op("vector", lambda e, ic=ic, pi=pi, b=b: e.tensor_scalar(
                    h2fm[pi][:, ic * 128:(ic + 1) * 128], pTc[:, ic * 128:(ic + 1) * 128],
                    cb1[:, ic * 2 + b: ic * 2 + b + 1],
                    sb_f[:, ic * 2 + b: ic * 2 + b + 1],
                    ALU.mult, ALU.add), inc=(sV, 1) if ic == 7 else None)
            pTc_ev[0] = sV.n

        def p3_pre_gate(i):
            pi = i % 2
            p.w("tensor", sV)
            p.w("tensor", sV, v=pG_ev[0])
            for ic in range(8):
                p.op("tensor", lambda e, ic=ic, pi=pi: e.matmul(
                    pG[:, :], h2fm[pi][:, ic * 128:(ic + 1) * 128],
                    gw_sb[:, ic * 8:(ic + 1) * 8],
                    start=(ic == 0), stop=(ic == 7)), inc=(sT, 1) if ic == 7 else None)
            p.w("vector", sT)
            p.op("vector", lambda e: e.max(out=top8[:, :], in_=pG[:, :]), inc=(sV, 1))
            p.w("vector", sV)
            p.op("vector", lambda e: e.tensor_scalar(
                m1[:, :], pG[:, :], top8[:, 0:1], None, ALU.is_equal))
            p.op("vector", lambda e: e.tensor_scalar(
                m2[:, :], pG[:, :], top8[:, 1:2], None, ALU.is_equal))
            p.op("vector", lambda e: e.tensor_sub(
                wcols[:, 0:1], top8[:, 1:2], top8[:, 0:1]), inc=(sV, 1))
            pG_ev[0] = sV.n
            p.w("scalar", sV)
            p.op("scalar", lambda e: e.activation(
                wcols[:, 0:1], wcols[:, 0:1], AF.Exp), inc=(sS, 1))

        def p3_pre_wf(i):
            pi = i % 2
            p.w("vector", sS)
            p.op("vector", lambda e: e.tensor_scalar(
                wcols[:, 1:2], wcols[:, 0:1], 1.0, None, ALU.add), inc=(sV, 1))
            p.w("vector", sV)
            p.op("vector", lambda e: e.reciprocal(wcols[:, 1:2], wcols[:, 1:2]),
                 inc=(sV, 1))
            p.w("vector", sV)
            p.op("vector", lambda e: e.tensor_scalar(
                wcols[:, 2:3], wcols[:, 1:2], 1.0, -1.0, ALU.subtract, ALU.mult),
                inc=(sV, 1))
            p.w("vector", sV)
            p.op("vector", lambda e: e.tensor_scalar(
                m1[:, :], m1[:, :], wcols[:, 1:2], None, ALU.mult), inc=(sV, 1))
            p.w("vector", sV)
            p.op("vector", lambda e: e.scalar_tensor_tensor(
                wf[:, :], m2[:, :], wcols[:, 2:3], m1[:, :], ALU.mult, ALU.add),
                inc=(sV, 1))
            vwf = sV.n
            p.w("tensor", sV, v=vwf)
            p.w("tensor", sV, v=pWf_ev[0])
            p.op("tensor", lambda e: e.transpose(pWf[:, :], wf[:, :], idf[:, :]),
                 inc=(sT, 1))
            p.w("vector", sT)
            p.op("vector", lambda e: e.tensor_copy(wft[:, :], pWf[:, :]), inc=(sV, 1))
            pWf_ev[0] = sV.n
            vwft = sV.n
            pe0, pe1 = pE[2 * (i % 2)], pE[2 * (i % 2) + 1]
            p.w("tensor", sV, v=vwft)
            p.w("tensor", sV, v=pE_ev[i % 2])
            for e_ in range(8):
                pe = (pe0, pe1)[e_ // 4]
                p.op("tensor", lambda e, e_=e_, pe=pe: e.matmul(
                    pe[:, (e_ % 4) * 128:(e_ % 4 + 1) * 128],
                    oneh[:, e_ * 128:(e_ + 1) * 128], wft[:, :],
                    start=True, stop=True), inc=(sT, 1) if e_ == 7 else None)
            p.w("vector", sT)
            p.w("vector", sV, v=wrep_cons[pi])
            p.op("vector", lambda e, pi=pi, pe0=pe0: e.tensor_copy(
                wrep[pi][:, 0:512], pe0[:, :]))
            p.op("vector", lambda e, pi=pi, pe1=pe1: e.tensor_copy(
                wrep[pi][:, 512:1024], pe1[:, :]), inc=(sV, 1))
            wrep_rdy[pi] = sV.n

        PRE_UNITS = [p3_pre_loads, p3_pre_ctxT, p3_pre_x1, p3_pre_h2T,
                     p3_pre_gate, p3_pre_wf]

        def p3_prework(i):
            for u in PRE_UNITS:
                u(i)

        def p3_scaled(i):
            pi = i % 2
            for e_ in range(8):
                p.w("vector", sV, v=wrep_rdy[pi])
                p.w("vector", sT, v=scl_cons[e_])
                wap = bass.AP(wrep[pi], e_ * 128, [[8 * 128, 128], [0, 8], [1, 128]])
                p.op("vector", lambda e, e_=e_, pi=pi, wap=wap: e.tensor_tensor(
                    scl[e_][:, :], h2fm[pi][:, :], wap, ALU.mult), inc=(sV, 1))
                scl_done[(i, e_)] = sV.n

        def p3_experts(i, nxt):
            pi = i % 2
            pe0, pe1 = pE[2 * (i % 2)], pE[2 * (i % 2) + 1]
            for e_ in range(8):
                si = e_
                p.w("tensor", sV, v=scl_done[(i, e_)])
                if e_ == 0:
                    if i == 0:
                        p.w("tensor", qEW)
                    p.w("tensor", sV, v=pE_ev[i % 2])
                for ic in range(8):
                    for oh in range(2):
                        pe = (pe0, pe1)[oh]
                        p.op("tensor", lambda e, e_=e_, ic=ic, oh=oh, si=si, pe=pe: e.matmul(
                            pe[:, :], scl[si][:, ic * 128:(ic + 1) * 128],
                            ew_sb[:, e_ * 8 * D + ic * D + oh * 512: e_ * 8 * D + ic * D + oh * 512 + 512],
                            start=(e_ == 0 and ic == 0), stop=(e_ == 7 and ic == 7)),
                            inc=(sT, 1) if (ic == 7 and oh == 1) else None)
                scl_cons[si] = sT.n
                if nxt is not None:
                    if e_ == 0:
                        PRE_UNITS[0](nxt)
                        PRE_UNITS[1](nxt)
                    elif e_ <= 4:
                        PRE_UNITS[e_ + 1](nxt)
            h2_cons[pi] = sT.n
            wrep_cons[pi] = sV.n
            texp_by[i] = sT.n

        def p3_evac(i):
            b = i // NT
            t0 = (i % NT) * 128
            pi = i % 2
            pe0, pe1 = pE[2 * (i % 2)], pE[2 * (i % 2) + 1]
            p.w("vector", sT, v=texp_by[i])
            for oh in range(2):
                pe = (pe0, pe1)[oh]
                p.w("vector", gO[oh], v=tmpb_st[oh])
                p.op("vector", lambda e, oh=oh, b=b, pe=pe: e.tensor_tensor(
                    tmpb2[oh][:, :], pe[:, :],
                    gbrep[:, b * D + oh * 512: b * D + (oh + 1) * 512], ALU.mult),
                    inc=(sV, 1))
                p.w("gpsimd", sV, v=sV.n)
                p.w("gpsimd", gX1[pi], v=x1_st[pi])
                p.op("gpsimd", lambda e, oh=oh, b=b, t0=t0: e.dma_start(
                    out=out_d[b, t0:t0 + 128, oh * 512:(oh + 1) * 512],
                    in_=tmpb2[oh][:, :], accum_op=ALU.add), inc=(gO[oh], 16))
                tmpb_st[oh] = gO[oh].n
            pE_ev[i % 2] = sV.n

        NTI = NB * NT
        p3_prework(0)
        for i in range(NTI):
            p3_scaled(i)
            if i > 0:
                p3_evac(i - 1)
            p3_experts(i, i + 1 if i + 1 < NTI else None)
        p3_evac(NTI - 1)

        p.w("gpsimd", gO[0])
        p.w("gpsimd", gO[1])
        p.flush()
        es3.close()

    return nc



def _consts():
    import ml_dtypes
    bf = ml_dtypes.bfloat16
    idb = np.eye(128, dtype=bf)
    idf = np.eye(128, dtype=np.float32)
    kpad2 = np.zeros((2, 128), np.float32)
    kpad2[0, 0:64] = NEG
    kpad2[1, 64:128] = NEG
    qpat = np.zeros((2, 768), np.float32)
    qpat[0, 64:256] = 1.0
    qpat[1, 128:256] = 1.0
    qpat[0, 256 + 192:512] = 1.0
    qpat[1, 256:256 + 64] = 1.0
    qpat[0, 512:512 + 128] = 1.0
    qpat[1, 512:512 + 192] = 1.0
    kpadB = np.zeros((2, 384), np.float32)
    kpadB[0, 64:128] = NEG
    kpadB[0, 128:192] = NEG
    kpadB[1, 192:256] = NEG
    kpadB[1, 256:320] = NEG
    qindB = np.zeros((2, 128), np.float32)
    qindB[0, 64:128] = 1.0
    qindB[1, 0:64] = 1.0
    oneh = np.zeros((8, 8 * 128), np.float32)
    for e in range(8):
        oneh[e, e * 128:(e + 1) * 128] = 1.0
    return dict(idb=idb, idf=idf, kpad2=kpad2.astype(bf), kpadB=kpadB.astype(bf),
                qpat=qpat.astype(bf), qindB=qindB.astype(bf), oneh=oneh.astype(bf))


def _layout_weights(inp):
    f32 = np.float32
    Wq = np.asarray(inp["L0_attn_Wq"], f32)
    Wk = np.asarray(inp["L0_attn_Wk"], f32)
    Wv = np.asarray(inp["L0_attn_Wv"], f32)
    gw = np.asarray(inp["L0_ffn_gate_w"], f32)
    ew = np.asarray(inp["L0_ffn_expert_w"], f32)
    wmod = np.asarray(inp["L0_wmod"], f32)

    def wfm(WT):
        return np.ascontiguousarray(WT.reshape(8, 128, -1).transpose(1, 0, 2).reshape(128, -1))

    d = {}
    d["wq"] = wfm(np.ascontiguousarray(Wq.T))
    d["wk"] = wfm(np.ascontiguousarray(Wk.T))
    d["wv"] = wfm(np.ascontiguousarray(Wv.T))
    d["gw"] = wfm(np.ascontiguousarray(gw.T))
    d["ew"] = np.ascontiguousarray(np.stack([wfm(np.ascontiguousarray(ew[e].T))
                                             for e in range(8)]))
    d["wm"] = np.ascontiguousarray(
        wmod.reshape(8, 128, 48, 128).transpose(2, 1, 0, 3).reshape(48, 128, 1024))
    d["bq"] = np.ascontiguousarray(np.asarray(inp["L0_attn_bq"], f32).reshape(8, 128))
    d["bk"] = np.ascontiguousarray(np.asarray(inp["L0_attn_bk"], f32).reshape(8, 128))
    d["bv"] = np.ascontiguousarray(np.asarray(inp["L0_attn_bv"], f32).reshape(1, 1024))
    return d


def kernel(**inp):
    x = np.asarray(inp["L0_attn_hidden_states"], np.float32)
    c = np.asarray(inp["L0_c"], np.float32)
    shared = _layout_weights(inp)
    shared.update(_consts())
    nc = build()
    maps = []
    for ci in range(NCORES):
        m = dict(shared)
        m["x"] = np.ascontiguousarray(x[NB * ci:NB * (ci + 1)])
        m["c2"] = np.ascontiguousarray(c[NB * ci:NB * (ci + 1)])
        maps.append(m)
    res = run_bass_kernel_spmd(nc, maps, core_ids=list(range(NCORES)))
    return np.concatenate([np.asarray(r["out"]) for r in res.results], axis=0)


# version 26 (final)
# speedup vs baseline: 1.0595x; 1.0595x over previous
import numpy as np
from contextlib import ExitStack

import concourse.bass as bass
import concourse.mybir as mybir
from concourse.bass_utils import run_bass_kernel_spmd

F32 = mybir.dt.float32
BF16 = mybir.dt.bfloat16
AF = mybir.ActivationFunctionType
ALU = mybir.AluOpType
AX = mybir.AxisListType.X

NCORES = 8
NB = 2
S = 4096
D = 1024
H = 16
NEG = -30000.0
BCONST = 32.0
NT = S // 128
NST = S // 512

DBG = False


class Sem:
    def __init__(self, h):
        self.h = h
        self.n = 0


class Prog:
    ENGS = ("sync", "tensor", "vector", "scalar", "gpsimd")

    def __init__(self, nc, es):
        self.nc = nc
        self.es = es
        self.cur = None

    def sem(self, name):
        return Sem(self.es.enter_context(self.nc.semaphore(name)))

    def new_block(self):
        assert self.cur is None
        self.cur = {e: [] for e in self.ENGS}

    def op(self, eng, fn, inc=None):
        if inc is not None:
            inc[0].n += inc[1]
        self.cur[eng].append((fn, inc))

    def w(self, eng, sem, v=None):
        val = sem.n if v is None else v
        if val <= 0:
            return
        self.cur[eng].append((lambda e, s=sem.h, x=val: e.wait_ge(s, x), None))

    def flush(self):
        blkops = self.cur
        self.cur = None
        with self.nc.Block() as blk:
            for eng in self.ENGS:
                ops = blkops[eng]
                if not ops:
                    continue
                def body(e, ops=ops):
                    for fn, inc in ops:
                        r = fn(e)
                        if inc is not None:
                            r.then_inc(inc[0].h, inc[1])
                getattr(blk, eng)(body)


def build(dbg=DBG):
    nc = bass.Bass("TRN2", target_bir_lowering=False)
    dt_scr = "ExternalOutput" if dbg else "Internal"

    x_d = nc.dram_tensor("x", [NB, S, D], F32, kind="ExternalInput").ap()
    wq_d = nc.dram_tensor("wq", [128, 8 * D], F32, kind="ExternalInput").ap()
    wk_d = nc.dram_tensor("wk", [128, 8 * D], F32, kind="ExternalInput").ap()
    wv_d = nc.dram_tensor("wv", [128, 8 * D], F32, kind="ExternalInput").ap()
    bq_d = nc.dram_tensor("bq", [8, 128], F32, kind="ExternalInput").ap()
    bk_d = nc.dram_tensor("bk", [8, 128], F32, kind="ExternalInput").ap()
    bv_d = nc.dram_tensor("bv", [1, D], F32, kind="ExternalInput").ap()
    gw_d = nc.dram_tensor("gw", [128, 64], F32, kind="ExternalInput").ap()
    ew_d = nc.dram_tensor("ew", [8, 128, 8 * D], F32, kind="ExternalInput").ap()
    c_d = nc.dram_tensor("c2", [NB, D], F32, kind="ExternalInput").ap()
    wm_d = nc.dram_tensor("wm", [48, 128, D], F32, kind="ExternalInput").ap()
    idb_d = nc.dram_tensor("idb", [128, 128], BF16, kind="ExternalInput").ap()
    idf_d = nc.dram_tensor("idf", [128, 128], F32, kind="ExternalInput").ap()
    kpad2_d = nc.dram_tensor("kpad2", [2, 128], BF16, kind="ExternalInput").ap()
    kpadB_d = nc.dram_tensor("kpadB", [2, 384], BF16, kind="ExternalInput").ap()
    qpat_d = nc.dram_tensor("qpat", [2, 768], BF16, kind="ExternalInput").ap()
    qindB_d = nc.dram_tensor("qindB", [2, 128], BF16, kind="ExternalInput").ap()
    oneh_d = nc.dram_tensor("oneh", [8, 8 * 128], BF16, kind="ExternalInput").ap()

    out_d = nc.dram_tensor("out", [NB, S, D], F32, kind="ExternalOutput").ap()

    qs_d = nc.dram_tensor("qs", [NB, D, S], BF16, kind=dt_scr).ap()
    ks_d = nc.dram_tensor("ks", [NB, D, S], BF16, kind=dt_scr).ap()
    vs_d = nc.dram_tensor("vs", [NB, S, D], BF16, kind=dt_scr).ap()
    cx_d = nc.dram_tensor("cx", [NB, 32, 2, 64, 8, 128], BF16, kind=dt_scr).ap()
    cs_d = nc.dram_tensor("cs", [NB, 32, H, 128], BF16, kind=dt_scr).ap()
    gbr_d = nc.dram_tensor("gbr", [NB, D], F32, kind="Internal").ap()
    dbg_d = nc.dram_tensor("dbg", [65, 64], F32, kind=dt_scr).ap() if dbg else None
    wts_d = nc.dram_tensor("wts", [NB * NT, 8, 128], F32, kind="Internal").ap()

    es = ExitStack()
    with es:
        p = Prog(nc, es)
        SB = lambda st, name, shp, dt: st.enter_context(nc.sbuf_tensor("s_" + name, shp, dt))
        PS = lambda st, name, shp, dt: st.enter_context(nc.psum_tensor("p_" + name, shp, dt))

        idb = SB(es, "idb", [128, 128], BF16)
        idf = SB(es, "idf", [128, 128], F32)
        cb1 = SB(es, "cb1", [128, 16], F32)
        sb_f = SB(es, "sb_f", [128, 16], F32)
        gaE = SB(es, "gaE", [65, 32], F32)
        bqc = SB(es, "bqc", [128, 8], F32)
        bkc = SB(es, "bkc", [128, 8], F32)
        oneh = SB(es, "oneh", [8, 8 * 128], BF16)
        gw_sb = SB(es, "gw_sb", [128, 64], BF16)
        cst = SB(es, "cst", [128, 2], F32)
        es01 = ExitStack()
        ca1 = SB(es01, "ca1", [128, 16], F32)
        sa_f = SB(es01, "sa_f", [128, 16], F32)
        nc.const_aps.register(F32, 1e-5, cst[:, 0:1])
        nc.const_aps.register(F32, -BCONST, cst[:, 1:2])

        sT = p.sem("sT")
        sV = p.sem("sV")
        sS = p.sem("sS")
        qW = p.sem("qW")
        qG = p.sem("qG")
        qM = p.sem("qM")
        qX = [p.sem(f"qX{i}") for i in range(16)]
        qQK = [p.sem(f"qQK{i}") for i in range(2)]
        qV_ = [p.sem(f"qV{i}") for i in range(2)]
        qVH = [p.sem(f"qVH{i}") for i in range(2)]
        gS1 = p.sem("gS1")
        gQ = [p.sem(f"gQ{i}") for i in range(2)]
        gK = [p.sem(f"gK{i}") for i in range(2)]
        gV2 = [p.sem(f"gV2{i}") for i in range(2)]
        gS2 = [p.sem(f"gS2{i}") for i in range(2)]
        gO = [p.sem(f"gO{i}") for i in range(2)]
        qEW = p.sem("qEW")
        qC = [p.sem(f"qC{i}") for i in range(2)]
        qPX = [p.sem(f"qPX{i}") for i in range(4)]
        sWF = p.sem("sWF")
        qWR = [p.sem(f"qWR{i}") for i in range(2)]
        qGB = p.sem("qGB")
        gX1 = [p.sem(f"gX1{i}") for i in range(2)]
        sOUT = p.sem("sOUT")

        p.new_block()
        es0 = ExitStack()
        c_sb = SB(es0, "c_sb", [NB, D], F32)
        scfm = SB(es0, "scfm", [128, 16], F32)
        wmb = [SB(es0, f"wmb{i}", [128, D], F32) for i in range(2)]
        en = SB(es0, "en", [NB, D], F32)
        sig = SB(es0, "sig", [NB, D], F32)
        silu = SB(es0, "silu", [NB, D], F32)
        modf = SB(es0, "modf", [128, 96], F32)
        gaf = SB(es0, "gaf", [128, 16], F32)
        gbf = SB(es0, "gbf", [128, 16], F32)
        garow = [SB(es0, f"garow{b}", [1, D], F32) for b in range(NB)]
        gbrow = [SB(es0, f"gbrow{b}", [1, D], F32) for b in range(NB)]
        b8 = SB(es0, "b8", [8, 128], F32)
        pm0 = PS(es0, "pm0", [128, 16], F32)
        pmm_ = [PS(es0, f"pmm{i}", [128, 2], F32) for i in range(2)]
        pm3 = PS(es0, "pm3", [1, D], F32)
        pm4 = PS(es0, "pm4", [128, 8], F32)
        pm5 = PS(es0, "pm5", [64, 32], F32)

        for src, dst in [(idb_d, idb), (idf_d, idf), (oneh_d, oneh),
                         (c_d, c_sb), (bq_d, b8)]:
            p.op("sync", lambda e, s=src, d=dst: e.dma_start(out=d[:, :], in_=s[:, :]),
                 inc=(qW, 16))
        p.op("gpsimd", lambda e: e.dma_start(out=gw_sb[:, :], in_=gw_d[:, :]),
             inc=(qG, 16))

        p.op("vector", lambda e: e.memset(cst[:, 0:1], 1e-5))
        p.op("vector", lambda e: e.memset(cst[:, 1:2], -BCONST), inc=(sV, 1))

        p.w("scalar", qW)
        p.op("scalar", lambda e: e.activation(en[:, :], c_sb[:, :], AF.Exp, scale=-1.0),
             inc=(sS, 1))
        p.w("vector", sS)
        p.op("vector", lambda e: e.tensor_scalar(sig[:, :], en[:, :], 1.0, None, ALU.add),
             inc=(sV, 1))
        p.w("vector", sV)
        p.op("vector", lambda e: e.reciprocal(sig[:, :], sig[:, :]), inc=(sV, 1))
        p.w("vector", sV)
        p.op("vector", lambda e: e.tensor_mul(silu[:, :], c_sb[:, :], sig[:, :]),
             inc=(sV, 1))
        p.w("tensor", sV)
        p.w("tensor", qW)
        for ic in range(8):
            p.op("tensor", lambda e, ic=ic: e.transpose(
                pm0[:, ic * 2:(ic + 1) * 2], silu[:, ic * 128:(ic + 1) * 128],
                idf[0:NB, 0:NB]), inc=(sT, 1))
        p.w("vector", sT)
        p.op("vector", lambda e: e.tensor_copy(scfm[:, :], pm0[:, :]), inc=(sV, 1))
        vc_scfm = sV.n

        tdone_oc = {}
        for oc in range(48):
            buf = wmb[oc % 2]
            pmm = pmm_[oc % 2]
            if oc >= 2:
                p.w("sync", sT, v=tdone_oc[oc - 2])
            p.op("sync", lambda e, oc=oc, b=buf: e.dma_start(out=b[:, :], in_=wm_d[oc]),
                 inc=(qM, 16))
            p.w("tensor", qM)
            p.w("tensor", sV, v=vc_scfm)
            if oc >= 2:
                p.w("tensor", sV)
            for ic in range(8):
                p.op("tensor", lambda e, b=buf, ic=ic, pmm=pmm: e.matmul(
                    pmm[:, :], b[:, ic * 128:(ic + 1) * 128],
                    scfm[:, ic * 2:(ic + 1) * 2],
                    start=(ic == 0), stop=(ic == 7)),
                    inc=(sT, 1) if ic == 7 else None)
            tdone_oc[oc] = sT.n
            p.w("vector", sT)
            p.op("vector", lambda e, oc=oc, pmm=pmm: e.tensor_copy(
                modf[:, oc * 2:(oc + 1) * 2], pmm[:, :]), inc=(sV, 1))

        p.w("vector", sV)
        p.op("vector", lambda e: e.tensor_scalar(ca1[:, :], modf[:, 16:32], 1.0, None, ALU.add))
        p.op("vector", lambda e: e.tensor_copy(sa_f[:, :], modf[:, 0:16]))
        p.op("vector", lambda e: e.tensor_scalar(cb1[:, :], modf[:, 64:80], 1.0, None, ALU.add))
        p.op("vector", lambda e: e.tensor_copy(sb_f[:, :], modf[:, 48:64]))
        gtmp = SB(es0, "gtmp", [128, 16], F32)
        for g_src, g_dst in [(32, gaf), (80, gbf)]:
            p.op("vector", lambda e, s=g_src, d=g_dst: e.tensor_scalar(
                d[:, :], modf[:, s:s + 16], 0.0, None, ALU.is_ge), inc=(sV, 1))
            p.w("vector", sV)
            p.op("vector", lambda e, d=g_dst: e.tensor_scalar(
                d[:, :], d[:, :], 2.0, -1.0, ALU.mult, ALU.add), inc=(sV, 1))
            p.op("vector", lambda e, s=g_src: e.tensor_scalar(
                gtmp[:, :], modf[:, s:s + 16], -1.0, None, ALU.mult), inc=(sV, 1))
            p.w("vector", sV)
            p.op("vector", lambda e, s=g_src: e.tensor_max(
                gtmp[:, :], gtmp[:, :], modf[:, s:s + 16]), inc=(sV, 1))
            p.w("vector", sV)
            p.op("vector", lambda e: e.tensor_scalar_max(
                gtmp[:, :], gtmp[:, :], 0.25), inc=(sV, 1))
            p.w("vector", sV)
            p.op("vector", lambda e, d=g_dst: e.tensor_mul(
                d[:, :], d[:, :], gtmp[:, :]), inc=(sV, 1))
        vc_mod = sV.n

        p.w("tensor", sV, v=vc_mod)
        for src, rows in [(gaf, garow), (gbf, gbrow)]:
            for b in range(NB):
                for ic in range(8):
                    p.op("tensor", lambda e, ic=ic, src=src, b=b: e.transpose(
                        pm3[:, ic * 128:(ic + 1) * 128],
                        src[:, ic * 2 + b: ic * 2 + b + 1], idf[:, :]),
                        inc=(sT, 1) if ic == 7 else None)
                p.w("vector", sT)
                p.op("vector", lambda e, rows=rows, b=b: e.tensor_copy(
                    rows[b][:, :], pm3[:, :]), inc=(sV, 1))
                p.w("tensor", sV)
        p.w("gpsimd", sV)
        for b in range(NB):
            p.op("gpsimd", lambda e, b=b: e.dma_start(
                out=gbr_d[b:b + 1, :], in_=gbrow[b][:, :]), inc=(qG, 16))
        for b in range(NB):
            for h in range(H):
                p.op("tensor", lambda e, b=b, h=h: e.transpose(
                    pm5[:, (b * H + h):(b * H + h) + 1],
                    garow[b][0:1, h * 64:(h + 1) * 64], idf[0:1, 0:1]),
                    inc=(sT, 1) if (b == NB - 1 and h == H - 1) else None)
        p.w("vector", sT)
        p.op("vector", lambda e: e.tensor_copy(gaE[0:64, :], pm5[:, :]))
        p.op("vector", lambda e: e.memset(gaE[64:65, :], 1.0), inc=(sV, 1))

        p.w("tensor", qW)
        p.op("tensor", lambda e: e.transpose(pm4[:, :], b8[:, :], idf[0:8, 0:8]),
             inc=(sT, 1))
        tq_b8 = sT.n
        p.w("vector", sT)
        p.op("vector", lambda e: e.tensor_copy(bqc[:, :], pm4[:, :]), inc=(sV, 1))
        p.w("sync", sT, v=tq_b8)
        p.op("sync", lambda e: e.dma_start(out=b8[:, :], in_=bk_d[:, :]), inc=(qW, 16))
        p.w("tensor", qW)
        p.w("tensor", sV)
        p.op("tensor", lambda e: e.transpose(pm4[:, :], b8[:, :], idf[0:8, 0:8]),
             inc=(sT, 1))
        p.w("vector", sT)
        p.op("vector", lambda e: e.tensor_copy(bkc[:, :], pm4[:, :]), inc=(sV, 1))

        if dbg:
            p.w("gpsimd", sV)
            p.op("gpsimd", lambda e: e.dma_start(out=dbg_d[:, 0:32], in_=gaE[:, :]),
                 inc=(qG, 16))
            p.op("gpsimd", lambda e: e.dma_start(
                out=dbg_d[0:1, 32:48],
                in_=bass.AP(garow[0], 0, [[64, 1], [1, 16]])), inc=(qG, 16))
            p.w("gpsimd", qG)
        p.flush()
        es0.close()

        p.new_block()
        es1 = ExitStack()
        wq_sb = SB(es1, "wq_sb", [128, 8 * D], BF16)
        wk_sb = SB(es1, "wk_sb", [128, 8 * D], BF16)
        wv_sb = SB(es1, "wv_sb", [128, 8 * D], BF16)
        bvr = SB(es1, "bvr", [128, D], F32)
        xbuf = [SB(es1, f"xbuf{i}", [128, D], F32) for i in range(16)]
        tb = [SB(es1, f"tb{i}", [128, D], BF16) for i in range(4)]
        junk = SB(es1, "junk", [128, D], BF16)
        stat = SB(es1, "stat", [128, 8], F32)
        hfm = [SB(es1, f"hfm{i}", [128, 8 * 512], BF16) for i in range(2)]
        qsb = [SB(es1, f"qsb{i}", [128, 512], BF16) for i in range(2)]
        ksb = [SB(es1, f"ksb{i}", [128, 512], BF16) for i in range(2)]
        vsb = [SB(es1, f"vsb{i}", [128, 512], BF16) for i in range(2)]
        pt1 = PS(es1, "pt1", [128, 1024], BF16)
        pq = [PS(es1, f"pq{i}", [128, 512], F32) for i in range(2)]
        pk = [PS(es1, f"pk{i}", [128, 512], F32) for i in range(2)]
        pv = [PS(es1, f"pv{i}", [128, 512], F32) for i in range(2)]

        for src, dst in [(wq_d, wq_sb), (wk_d, wk_sb), (wv_d, wv_sb)]:
            p.op("gpsimd", lambda e, s=src, d=dst: e.dma_start(out=d[:, :], in_=s[:, :]),
                 inc=(qG, 16))
        p.op("gpsimd", lambda e: e.dma_start(
            out=bvr[:, :], in_=bass.AP(bv_d.tensor, 0, [[0, 128], [1, D]])),
            inc=(qG, 16))
        qg_w = qG.n

        x_cons = [0] * 16
        t_cons = [0, 0, 0, 0]
        pq_ev = [0, 0]
        pk_ev = [0, 0]
        pv_ev = [0, 0]
        ptr_ev = [0]
        last_var = [0]
        hfm_cons = [0, 0]
        hfm_done = [0, 0]
        NSUP = NB * NST

        def p1_tilework(s_, taus=range(4)):
            b, st_ = divmod(s_, NST)
            t0 = st_ * 512
            hf = hfm[s_ % 2]
            for tau in taus:
                bi = (s_ * 4 + tau) % 16
                ti = (s_ * 4 + tau) % 4
                tt0 = t0 + tau * 128
                p.w("sync", sV, v=x_cons[bi])
                p.op("sync", lambda e, b=b, tt0=tt0, bi=bi: e.dma_start(
                    out=xbuf[bi][:, :], in_=x_d[b, tt0:tt0 + 128, :]),
                    inc=(qX[bi], 16))
                p.w("vector", qX[bi])
                p.op("vector", lambda e, bi=bi: e.reduce_sum(
                    stat[:, 0:1], xbuf[bi][:, :], axis=AX), inc=(sV, 1))
                p.w("scalar", qX[bi])
                p.w("scalar", sV, v=last_var[0])
                p.op("scalar", lambda e, bi=bi: e.activation(
                    junk[:, :], xbuf[bi][:, :], AF.Square,
                    accum_out=stat[:, 1:2]), inc=(sS, 1))
                p.w("vector", sV)
                p.w("vector", sS)
                p.op("vector", lambda e: e.tensor_scalar(
                    stat[:, 2:3], stat[:, 0:1], 1.0 / D, None, ALU.mult), inc=(sV, 1))
                p.w("vector", sV)
                p.op("vector", lambda e: e.tensor_mul(
                    stat[:, 3:4], stat[:, 2:3], stat[:, 2:3]), inc=(sV, 1))
                p.w("vector", sV)
                p.op("vector", lambda e: e.scalar_tensor_tensor(
                    stat[:, 4:5], stat[:, 1:2], 1.0 / D, stat[:, 3:4],
                    ALU.mult, ALU.subtract), inc=(sV, 1))
                last_var[0] = sV.n
                p.w("scalar", sV)
                p.op("scalar", lambda e: e.activation(
                    stat[:, 5:6], stat[:, 4:5], AF.Ln, bias=1e-5), inc=(sS, 1))
                p.w("scalar", sS)
                p.op("scalar", lambda e: e.activation(
                    stat[:, 6:7], stat[:, 5:6], AF.Exp, scale=-0.5), inc=(sS, 1))
                p.w("vector", sS)
                p.w("vector", sT, v=t_cons[ti])
                p.op("vector", lambda e, bi=bi, ti=ti: e.tensor_scalar(
                    tb[ti][:, :], xbuf[bi][:, :], stat[:, 2:3], stat[:, 6:7],
                    ALU.subtract, ALU.mult), inc=(sV, 1))
                x_cons[bi] = sV.n
                vt = sV.n
                p.w("tensor", sV, v=vt)
                p.w("tensor", sV, v=ptr_ev[0])
                for ic in range(8):
                    p.op("tensor", lambda e, ti=ti, ic=ic: e.transpose(
                        pt1[:, ic * 128:(ic + 1) * 128],
                        tb[ti][:, ic * 128:(ic + 1) * 128], idb[:, :]),
                        inc=(sT, 1) if ic == 7 else None)
                t_cons[ti] = sT.n
                p.w("vector", sT)
                p.w("vector", sT, v=hfm_cons[s_ % 2])
                for ic in range(8):
                    p.op("vector", lambda e, ic=ic, tau=tau, b=b, hf=hf: e.tensor_scalar(
                        hf[:, ic * 512 + tau * 128: ic * 512 + (tau + 1) * 128],
                        pt1[:, ic * 128:(ic + 1) * 128],
                        ca1[:, ic * 2 + b: ic * 2 + b + 1],
                        sa_f[:, ic * 2 + b: ic * 2 + b + 1],
                        ALU.mult, ALU.add), inc=(sV, 1) if ic == 7 else None)
                ptr_ev[0] = sV.n
            hfm_done[s_ % 2] = sV.n

        def p1_mms(s_, part):
            b, st_ = divmod(s_, NST)
            t0 = st_ * 512
            hf = hfm[s_ % 2]
            vh = hfm_done[s_ % 2]
            for wsb, bias, psm, pev, osb, gsem, scratch in (
                    ((wq_sb, bqc, pq, pq_ev, qsb, gQ, qs_d),
                     (wk_sb, bkc, pk, pk_ev, ksb, gK, ks_d))[part:part + 1]):
                for oc in range(8):
                    pi = oc % 2
                    pm = psm[pi]
                    p.w("tensor", sS, v=pev[pi])
                    p.w("tensor", sV, v=vh)
                    if s_ == 0 and oc == 0:
                        p.w("tensor", qG, v=qg_w)
                    for ic in range(8):
                        p.op("tensor", lambda e, ic=ic, oc=oc, wsb=wsb, pm=pm, hf=hf: e.matmul(
                            pm[:, :],
                            wsb[:, ic * D + oc * 128: ic * D + (oc + 1) * 128],
                            hf[:, ic * 512:(ic + 1) * 512],
                            start=(ic == 0), stop=(ic == 7)),
                            inc=(sT, 1) if ic == 7 else None)
                    p.w("scalar", sT, v=sT.n)
                    p.w("scalar", gsem[pi], v=gsem[pi].n)
                    p.op("scalar", lambda e, pm=pm, pi=pi, oc=oc, osb=osb, bias=bias: e.activation(
                        osb[pi][:, :], pm[:, :], AF.Identity,
                        bias=bias[:, oc:oc + 1]), inc=(sS, 1))
                    pev[pi] = sS.n
                    p.w("gpsimd", sS, v=sS.n)
                    p.op("gpsimd", lambda e, b=b, oc=oc, t0=t0, pi=pi, osb=osb, scratch=scratch: e.dma_start(
                        out=scratch[b, oc * 128:(oc + 1) * 128, t0:t0 + 512],
                        in_=osb[pi][:, :]), inc=(gsem[pi], 16))
            if part != 2:
                return
            for tau in range(4):
                for oh in range(2):
                    pi = (tau * 2 + oh) % 2
                    pm = pv[pi]
                    p.w("tensor", sV, v=pv_ev[pi])
                    for ic in range(8):
                        p.op("tensor", lambda e, ic=ic, tau=tau, oh=oh, pm=pm, hf=hf: e.matmul(
                            pm[:, :],
                            hf[:, ic * 512 + tau * 128: ic * 512 + (tau + 1) * 128],
                            wv_sb[:, ic * D + oh * 512: ic * D + (oh + 1) * 512],
                            start=(ic == 0), stop=(ic == 7)),
                            inc=(sT, 1) if ic == 7 else None)
                    p.w("vector", sT, v=sT.n)
                    p.w("vector", gV2[pi], v=gV2[pi].n)
                    p.op("vector", lambda e, pm=pm, pi=pi, oh=oh: e.scalar_tensor_tensor(
                        vsb[pi][:, :], pm[:, :], 1.0,
                        bvr[:, oh * 512:(oh + 1) * 512], ALU.mult, ALU.add),
                        inc=(sV, 1))
                    pv_ev[pi] = sV.n
                    p.w("gpsimd", sV, v=sV.n)
                    p.op("gpsimd", lambda e, b=b, t0=t0, tau=tau, oh=oh, pi=pi: e.dma_start(
                        out=vs_d[b, t0 + tau * 128: t0 + (tau + 1) * 128,
                                 oh * 512:(oh + 1) * 512],
                        in_=vsb[pi][:, :]), inc=(gV2[pi], 16))
            hfm_cons[s_ % 2] = sT.n

        p1_tilework(0)
        for s_ in range(NSUP):
            p1_mms(s_, 2)
            if s_ + 1 < NSUP:
                p1_tilework(s_ + 1, (0, 1))
            p1_mms(s_, 0)
            if s_ + 1 < NSUP:
                p1_tilework(s_ + 1, (2, 3))
            p1_mms(s_, 1)

        for sem in (gQ[0], gQ[1], gK[0], gK[1], gV2[0], gV2[1]):
            p.w("gpsimd", sem)
        p.flush()
        es1.close()
        es01.close()

        ew_sb = SB(es, "ew_sb", [128, 8 * 8 * D], BF16)

        p.new_block()
        es2 = ExitStack()
        qb2 = [SB(es2, f"qb2{i}", [128, S], BF16) for i in range(2)]
        kb2 = [SB(es2, f"kb2{i}", [128, S], BF16) for i in range(2)]
        vbL = [SB(es2, f"vbL{i}", [128, 32 * 65], BF16) for i in range(2)]
        vbH = [SB(es2, f"vbH{i}", [128, 33 * 65], BF16) for i in range(2)]
        expA = [SB(es2, f"expA{i}", [128, 1024], BF16) for i in range(2)]
        expM = [SB(es2, f"expM{i}", [128, 1024], BF16) for i in range(4)]
        cxs = [SB(es2, f"cxs{i}", [65, S], BF16) for i in range(2)]
        pSS = [PS(es2, f"pSS{i}", [128, 1024], F32) for i in range(3)]
        pC = [PS(es2, f"pC{i}", [65, 512], F32) for i in range(2)]

        for e_ in range(8):
            p.op("gpsimd", lambda e, e_=e_: e.dma_start(
                out=ew_sb[:, e_ * 8 * D:(e_ + 1) * 8 * D], in_=ew_d[e_]),
                inc=(qEW, 16))

        for i in range(2):
            p.op("vector", lambda e, i=i: e.memset(
                bass.AP(vbL[i], 64, [[32 * 65, 128], [65, 32], [1, 1]]), 1.0))
            p.op("vector", lambda e, i=i: e.memset(
                bass.AP(vbH[i], 64, [[33 * 65, 128], [65, 33], [1, 1]]), 1.0),
                inc=(sV, 1))
        vones = sV.n

        qk_cons = [0, 0]
        vbL_cons = [0, 0]
        vbH_cons = [0, 0]
        cx_st = [0, 0]
        expA_cons = [0, 0]
        pA_ev = [0, 0]
        pS3_ev = [[0, 0], [0, 0], [0, 0]]
        em4_cons = [0, 0, 0, 0]
        pc_last = 0
        emB_cons = [0]
        pC_ev = [[0, 0], [0, 0]]

        for b in range(NB):
            for pp in range(H // 2):
                pi = pp % 2
                cols = [b * H + 2 * pp, b * H + 2 * pp + 1]
                p.w("sync", sT, v=qk_cons[pi])
                for h01 in range(2):
                    h = 2 * pp + h01
                    p.op("sync", lambda e, b=b, h=h, pi=pi, h01=h01: e.dma_start(
                        out=bass.AP(qb2[pi], h01 * 64 * S, [[S, 64], [1, S]]),
                        in_=qs_d[b, h * 64:(h + 1) * 64, :]), inc=(qQK[pi], 16))
                    p.op("sync", lambda e, b=b, h=h, pi=pi, h01=h01: e.dma_start(
                        out=bass.AP(kb2[pi], h01 * 64 * S, [[S, 64], [1, S]]),
                        in_=ks_d[b, h * 64:(h + 1) * 64, :]), inc=(qQK[pi], 16))
                for h01 in range(2):
                    h = 2 * pp + h01
                    p.w("sync", sT, v=vbL_cons[h01])
                    p.op("sync", lambda e, b=b, h=h, h01=h01: e.dma_start(
                        out=bass.AP(vbL[h01], 0, [[32 * 65, 128], [65, 32], [1, 64]]),
                        in_=bass.AP(vs_d.tensor, b * S * D + h * 64,
                                    [[D, 128], [128 * D, 32], [1, 64]])),
                        inc=(qV_[h01], 16))
                for h01 in range(2):
                    h = 2 * pp + h01
                    p.w("sync", sT, v=vbH_cons[h01])
                    p.op("sync", lambda e, b=b, h=h, h01=h01: e.dma_start(
                        out=bass.AP(vbH[h01], 0, [[33 * 65, 128], [65, 31], [1, 64]]),
                        in_=bass.AP(vs_d.tensor, b * S * D + 64 * D + h * 64,
                                    [[D, 128], [128 * D, 31], [1, 64]])),
                        inc=(qVH[h01], 16))
                    for r, (t0f, t0b) in enumerate([(0, 128), (S - 64, S - 192)]):
                        p.op("sync", lambda e, b=b, h=h, h01=h01, r=r, t0f=t0f: e.dma_start(
                            out=bass.AP(vbH[h01], 31 * 65 + r * 64 * 33 * 65,
                                        [[33 * 65, 64], [1, 64]]),
                            in_=bass.AP(vs_d.tensor, b * S * D + t0f * D + h * 64,
                                        [[D, 64], [1, 64]])),
                            inc=(qVH[h01], 16))
                        p.op("sync", lambda e, b=b, h=h, h01=h01, r=r, t0b=t0b: e.dma_start(
                            out=bass.AP(vbH[h01], 32 * 65 + r * 64 * 33 * 65,
                                        [[33 * 65, 64], [1, 64]]),
                            in_=bass.AP(vs_d.tensor, b * S * D + t0b * D + h * 64,
                                        [[D, 64], [1, 64]])),
                            inc=(qVH[h01], 16))

                p.w("tensor", qQK[pi])
                p.w("tensor", qV_[0])
                p.w("tensor", qV_[1])
                if b == 0 and pp == 0:
                    p.w("tensor", sV, v=vones)

                qA = [bass.AP(qb2[pi], h01 * 64 * S, [[S, 64], [4032, 2], [1, 64]])
                      for h01 in range(2)]
                qB_ = [bass.AP(qb2[pi], h01 * 64 * S + 64, [[S, 64], [3904, 2], [1, 64]])
                       for h01 in range(2)]
                ksl = [lambda c0, c1, h01=h01: bass.AP(
                    kb2[pi], h01 * 64 * S + c0, [[S, 64], [1, c1 - c0]])
                    for h01 in range(2)]

                for g in range(9):
                    if g < 8:
                        bank = pSS[g % 2]
                        p.w("tensor", sS, v=pA_ev[g % 2])
                        if g < 2:
                            p.w("tensor", sS, v=pS3_ev[g % 2][0])
                            p.w("tensor", sS, v=pS3_ev[g % 2][1])
                        for j in range(4):
                            c = g * 4 + j
                            for h01 in range(2):
                                p.op("tensor", lambda e, c=c, j=j, bank=bank, pi=pi,
                                     h01=h01, qA=qA: e.matmul(
                                    bank[:, h01 * 512 + j * 128: h01 * 512 + (j + 1) * 128],
                                    bass.AP(kb2[pi], h01 * 64 * S + c * 128,
                                            [[S, 64], [1, 128]]),
                                    qA[h01], start=True, stop=True),
                                    inc=(sT, 1) if (j == 3 and h01 == 1) else None)
                        tdata = sT.n
                        p.w("scalar", sT, v=tdata)
                        p.w("scalar", sT, v=expA_cons[g % 2])
                        p.op("scalar", lambda e, g=g, bank=bank: e.activation(
                            expA[g % 2][:, :], bank[:, :],
                            AF.Exp, scale=0.125, bias=-BCONST), inc=(sS, 1))
                        pA_ev[g % 2] = sS.n
                    if g > 0:
                        gg = g - 1
                        p.w("tensor", sS, v=pA_ev[gg % 2])
                        if gg == 0:
                            p.w("tensor", sV, v=pc_last)
                        for h01 in range(2):
                            for j in range(4):
                                c = gg * 4 + j
                                p.op("tensor", lambda e, c=c, gg=gg, j=j, h01=h01: e.matmul(
                                    pC[h01][:, 0:128],
                                    vbL[h01][:, c * 65: c * 65 + 65],
                                    expA[gg % 2][:, h01 * 512 + j * 128: h01 * 512 + (j + 1) * 128],
                                    start=(c == 0), stop=(c == 31)),
                                    inc=(sT, 1) if (j == 3 and h01 == 1) else None)
                        expA_cons[gg % 2] = sT.n
                p.w("vector", sT)
                for h01 in range(2):
                    p.w("vector", gS2[h01], v=cx_st[h01])
                    p.op("vector", lambda e, h01=h01, col=cols[h01]: e.tensor_scalar(
                        bass.AP(cxs[h01], 0, [[S, 65], [4032, 2], [1, 64]]),
                        pC[h01][:, 0:128], gaE[:, col:col + 1], None, ALU.mult),
                        inc=(sV, 1))
                    pC_ev[h01][0] = sV.n
                    pc_last = sV.n

                bank = pSS[0]
                p.w("tensor", sS, v=pA_ev[0])
                for c in range(3):
                    for h01 in range(2):
                        if c == 1:
                            for r, c0 in enumerate((128, 3904)):
                                p.op("tensor", lambda e, c=c, h01=h01, r=r, c0=c0,
                                     bank=bank, qB_=qB_, pi=pi: e.matmul(
                                    bank[r * 64:(r + 1) * 64,
                                         h01 * 512 + c * 128: h01 * 512 + (c + 1) * 128],
                                    bass.AP(kb2[pi], h01 * 64 * S + c0,
                                            [[S, 64], [1, 64]]),
                                    qB_[h01], start=True, stop=True))
                            continue
                        c0 = 0 if c == 0 else 3968
                        kc = bass.AP(kb2[pi], h01 * 64 * S + c0, [[S, 64], [1, 128]])
                        p.op("tensor", lambda e, c=c, h01=h01, bank=bank, kc=kc, qB_=qB_: e.matmul(
                            bank[:, h01 * 512 + c * 128: h01 * 512 + (c + 1) * 128],
                            kc, qB_[h01], start=True, stop=True),
                            inc=(sT, 1) if (c == 2 and h01 == 1) else None)
                p.w("scalar", sT)
                p.w("scalar", sT, v=emB_cons[0])
                p.op("scalar", lambda e, bank=bank: e.activation(
                    expA[0][:, :], bank[:, :], AF.Exp, scale=0.125, bias=-BCONST),
                    inc=(sS, 1))
                pA_ev[0] = sS.n
                pS3_ev[0][0] = sS.n
                pS3_ev[0][1] = sS.n
                p.w("vector", sS)
                for h01 in range(2):
                    o = h01 * 512
                    p.op("vector", lambda e, o=o: e.memset(expA[0][64:128, o + 64:o + 128], 0.0))
                    p.op("vector", lambda e, o=o: e.memset(expA[0][0:64, o + 192:o + 256], 0.0))
                    p.op("vector", lambda e, o=o: e.memset(expA[0][64:128, o + 128:o + 192], 0.0))
                    p.op("vector", lambda e, o=o: e.memset(expA[0][0:64, o + 256:o + 320], 0.0),
                         inc=(sV, 1) if h01 == 1 else None)
                vmB = sV.n
                p.w("tensor", sV, v=vmB)
                p.w("tensor", sS)
                p.w("tensor", qVH[0])
                p.w("tensor", qVH[1])
                for h01 in range(2):
                    vchB = [(vbL, 0), (vbH, 32 * 65), (vbL, 31 * 65)]
                    for c in range(3):
                        vbuf, vco = vchB[c]
                        p.op("tensor", lambda e, c=c, h01=h01, vbuf=vbuf, vco=vco: e.matmul(
                            pC[h01][:, 0:128], vbuf[h01][:, vco:vco + 65],
                            expA[0][:, h01 * 512 + c * 128: h01 * 512 + (c + 1) * 128],
                            start=(c == 0), stop=(c == 2)),
                            inc=(sT, 1) if (c == 2 and h01 == 1) else None)
                emB_cons[0] = sT.n
                expA_cons[0] = sT.n
                vbL_cons[0] = sT.n
                vbL_cons[1] = sT.n
                p.w("vector", sT)
                for h01 in range(2):
                    p.op("vector", lambda e, h01=h01, col=cols[h01]: e.tensor_scalar(
                        bass.AP(cxs[h01], 64, [[S, 65], [3904, 2], [1, 64]]),
                        pC[h01][:, 0:128], gaE[:, col:col + 1], None, ALU.mult),
                        inc=(sV, 1))
                    pC_ev[h01][0] = sV.n
                    pc_last = sV.n

                vm = {}
                for g in range(16):
                    if g < 15:
                        q0 = (4 * g + 2) * 64
                        for h01 in range(2):
                            bi3 = (2 * g + h01) % 3
                            bi4 = (2 * g + h01) % 4
                            bk = pSS[bi3]
                            em = expM[bi4]
                            qsp = bass.AP(qb2[pi], h01 * 64 * S + q0, [[S, 64], [1, 256]])
                            p.w("tensor", sS, v=pS3_ev[bi3][0])
                            p.w("tensor", sS, v=pS3_ev[bi3][1])
                            for r, c0 in enumerate((0, 4032)):
                                p.op("tensor", lambda e, r=r, c0=c0, bk=bk, qsp=qsp,
                                     pi=pi, h01=h01: e.matmul(
                                    bk[r * 64:(r + 1) * 64, 0:256],
                                    bass.AP(kb2[pi], h01 * 64 * S + c0,
                                            [[S, 64], [1, 64]]),
                                    qsp, start=True, stop=True))
                            p.op("tensor", lambda e, g=g, bk=bk, qsp=qsp, pi=pi, h01=h01: e.matmul(
                                bk[:, 256:512],
                                bass.AP(kb2[pi], h01 * 64 * S + (4 * g + 1) * 64,
                                        [[S, 64], [1, 128]]),
                                qsp, start=True, stop=True), inc=(sT, 1))
                            t0g = sT.n
                            p.op("tensor", lambda e, g=g, bk=bk, qsp=qsp, pi=pi, h01=h01: e.matmul(
                                bk[:, 512:768],
                                bass.AP(kb2[pi], h01 * 64 * S + (4 * g + 3) * 64,
                                        [[S, 64], [1, 128]]),
                                qsp, start=True, stop=True))
                            p.op("tensor", lambda e, g=g, bk=bk, qsp=qsp, pi=pi, h01=h01: e.matmul(
                                bk[:, 768:1024],
                                bass.AP(kb2[pi], h01 * 64 * S + (4 * g + 5) * 64,
                                        [[S, 64], [1, 128]]),
                                qsp, start=True, stop=True), inc=(sT, 1))
                            t1g = sT.n
                            p.w("scalar", sT, v=t0g)
                            p.w("scalar", sT, v=em4_cons[bi4])
                            p.op("scalar", lambda e, bk=bk, em=em: e.activation(
                                em[:, 0:512], bk[:, 0:512], AF.Exp, scale=0.125,
                                bias=-BCONST), inc=(sS, 1))
                            pS3_ev[bi3][0] = sS.n
                            p.w("vector", sS)
                            p.op("vector", lambda e, em=em: e.memset(em[0:64, 256 + 64:512], 0.0))
                            p.op("vector", lambda e, em=em: e.memset(em[64:128, 256 + 128:512], 0.0),
                                 inc=(sV, 1))
                            vm[(g, h01, 0)] = sV.n
                            p.w("scalar", sT, v=t1g)
                            p.op("scalar", lambda e, bk=bk, em=em: e.activation(
                                em[:, 512:1024], bk[:, 512:1024], AF.Exp, scale=0.125,
                                bias=-BCONST), inc=(sS, 1))
                            pS3_ev[bi3][1] = sS.n
                            p.w("vector", sS)
                            p.op("vector", lambda e, em=em: e.memset(em[0:64, 512 + 192:768], 0.0))
                            p.op("vector", lambda e, em=em: e.memset(em[64:128, 512:512 + 64], 0.0))
                            p.op("vector", lambda e, em=em: e.memset(em[0:64, 768:768 + 128], 0.0))
                            p.op("vector", lambda e, em=em: e.memset(em[64:128, 768:768 + 192], 0.0),
                                 inc=(sV, 1))
                            vm[(g, h01, 1)] = sV.n
                    if g > 0:
                        gg = g - 1
                        reg = (gg % 2) * 256
                        p.w("tensor", sV, v=pc_last)
                        for h01 in range(2):
                            bi4 = (2 * gg + h01) % 4
                            em = expM[bi4]
                            pcb = pC[h01]
                            vch = [31 * 65, (2 * gg) * 65, (2 * gg + 1) * 65,
                                   (2 * gg + 2) * 65]
                            p.w("tensor", sV, v=vm[(gg, h01, 0)])
                            for c in range(2):
                                p.op("tensor", lambda e, c=c, pcb=pcb, reg=reg, h01=h01,
                                     vch=vch, em=em: e.matmul(
                                    pcb[:, reg:reg + 256], vbH[h01][:, vch[c]:vch[c] + 65],
                                    em[:, c * 256:(c + 1) * 256],
                                    start=(c == 0), stop=False))
                            p.w("tensor", sV, v=vm[(gg, h01, 1)])
                            for c in range(2, 4):
                                p.op("tensor", lambda e, c=c, pcb=pcb, reg=reg, h01=h01,
                                     vch=vch, em=em: e.matmul(
                                    pcb[:, reg:reg + 256], vbH[h01][:, vch[c]:vch[c] + 65],
                                    em[:, c * 256:(c + 1) * 256],
                                    start=False, stop=(c == 3)),
                                    inc=(sT, 1) if c == 3 else None)
                            em4_cons[bi4] = sT.n
                        p.w("vector", sT)
                        for h01 in range(2):
                            p.op("vector", lambda e, gg=gg, h01=h01, col=cols[h01],
                                 reg=reg: e.tensor_scalar(
                                cxs[h01][:, (4 * gg + 2) * 64:(4 * gg + 2) * 64 + 256],
                                pC[h01][:, reg:reg + 256], gaE[:, col:col + 1], None, ALU.mult),
                                inc=(sV, 1))
                            pC_ev[h01][gg % 2] = sV.n
                            pc_last = sV.n
                qk_cons[pi] = sT.n
                vbH_cons[0] = sT.n
                vbH_cons[1] = sT.n
                for h01 in range(2):
                    h = 2 * pp + h01
                    p.w("gpsimd", sV, v=pC_ev[h01][0])
                    hp_, pair_ = h % 2, h // 2
                    for dh in range(2):
                        p.op("gpsimd", lambda e, b=b, hp_=hp_, pair_=pair_, dh=dh, h01=h01: e.dma_start(
                            out=bass.AP(cx_d.tensor,
                                        b * 32 * 2 * 64 * 8 * 128 + hp_ * 64 * 8 * 128
                                        + dh * 32 * 8 * 128 + pair_ * 128,
                                        [[8 * 128, 32], [2 * 64 * 8 * 128, 32], [1, 128]]),
                            in_=bass.AP(cxs[h01], dh * 32 * S, [[S, 32], [128, 32], [1, 128]])),
                            inc=(gS2[h01], 16))
                    p.op("gpsimd", lambda e, b=b, h=h, h01=h01: e.dma_start(
                        out=bass.AP(cs_d.tensor, b * 32 * H * 128 + h * 128,
                                    [[32 * H * 128, 1], [H * 128, 32], [1, 128]]),
                        in_=bass.AP(cxs[h01], 64 * S, [[S, 1], [128, 32], [1, 128]])),
                        inc=(gS2[h01], 16))
                    cx_st[h01] = gS2[h01].n

        p.w("gpsimd", gS2[0])
        p.w("gpsimd", gS2[1])
        p.flush()
        es2.close()

        p.new_block()
        es3 = ExitStack()
        gbrep = SB(es3, "gbrep", [128, NB * D], BF16)
        px = [SB(es3, f"px{i}", [128, D], F32) for i in range(4)]
        cxp = [SB(es3, f"cxp{i}", [128, 8 * 128], BF16) for i in range(2)]
        sums = [SB(es3, f"sums{i}", [16, 128], BF16) for i in range(2)]
        sumf = SB(es3, "sumf", [128, 16], F32)
        rcp = [SB(es3, f"rcp{i}", [128, 16], F32) for i in range(2)]
        x1 = [SB(es3, f"x1{i}", [128, D], F32) for i in range(4)]
        tmpb2 = [SB(es3, f"tmpb2{i}", [128, 512], F32) for i in range(2)]
        tb2 = [SB(es3, f"tb2{i}", [128, D], BF16) for i in range(2)]
        stat2 = SB(es3, "stat2", [128, 8], F32)
        h2fm = [SB(es3, f"h2fm{i}", [128, 8 * 128], BF16) for i in range(2)]
        scl = [SB(es3, f"scl{i}", [128, 8 * 128], BF16) for i in range(8)]
        wrep = [SB(es3, f"wrep{i}", [128, 8 * 128], BF16) for i in range(2)]
        top8 = SB(es3, "top8", [128, 8], F32)
        m1 = SB(es3, "m1", [128, 8], F32)
        m2 = SB(es3, "m2", [128, 8], F32)
        wcols = SB(es3, "wcols", [128, 4], F32)
        wf = SB(es3, "wf", [128, 8], F32)
        wft = SB(es3, "wft", [8, 128], BF16)
        pTc = PS(es3, "pTc", [128, 1024], BF16)
        pE = [PS(es3, f"pE{i}", [128, 512], F32) for i in range(4)]
        pG = PS(es3, "pG", [128, 8], F32)
        pSm = PS(es3, "pSm", [128, 16], BF16)
        pWf = PS(es3, "pWf", [8, 128], F32)

        for b in range(NB):
            p.op("gpsimd", lambda e, b=b: e.dma_start(
                out=gbrep[:, b * D:(b + 1) * D],
                in_=bass.AP(gbr_d.tensor, b * D, [[0, 128], [1, D]])), inc=(qGB, 16))

        px_cons = [0, 0, 0, 0]
        x1_st = [0, 0]
        tmpb_st = [0, 0]
        cxp_cons = [0, 0]
        sums_cons = [0, 0]
        h2_cons = [0, 0]
        scl_cons = [0] * 8
        wrep_cons = [0, 0]
        scl_done = {}
        pTc_ev = [0]
        pE_ev = [0, 0]
        pG_ev = [0]
        pSm_ev = [0]
        pWf_ev = [0]
        h2t_done = [0, 0]
        wrep_rdy = [0, 0]
        texp_by = {}
        last_var2 = [0]

        def p3_pre_loads(i):
            b, tau = i // NT, i % NT
            t0 = tau * 128
            pi = i % 2
            px4 = i % 4
            p.w("sync", sV, v=px_cons[px4])
            p.op("sync", lambda e, b=b, t0=t0, px4=px4: e.dma_start(
                out=px[px4][:, :], in_=x_d[b, t0:t0 + 128, :]), inc=(qPX[px4], 16))
            p.w("sync", sT, v=max(cxp_cons[pi], sums_cons[pi]))
            p.op("sync", lambda e, b=b, tau=tau, pi=pi: e.dma_start(
                out=cxp[pi][:, :], in_=bass.AP(cx_d.tensor,
                    (b * 32 + tau) * 2 * 64 * 8 * 128, [[1024, 128], [1, 1024]])),
                inc=(qC[pi], 16))
            p.op("sync", lambda e, b=b, tau=tau, pi=pi: e.dma_start(
                out=sums[pi][:, :], in_=bass.AP(cs_d.tensor,
                    (b * 32 + tau) * H * 128, [[128, 16], [1, 128]])),
                inc=(qC[pi], 16))
            if i == 0:
                p.w("vector", qGB)

        def p3_pre_ctxT(i):
            pi = i % 2
            p.w("tensor", qC[pi])
            p.w("tensor", sV, v=pTc_ev[0])
            for r in range(8):
                p.op("tensor", lambda e, r=r, pi=pi: e.transpose(
                    pTc[:, r * 128:(r + 1) * 128], cxp[pi][:, r * 128:(r + 1) * 128],
                    idb[:, :]), inc=(sT, 1) if r == 7 else None)
            cxp_cons[pi] = sT.n
            p.w("tensor", sV, v=pSm_ev[0])
            p.op("tensor", lambda e, pi=pi: e.transpose(
                pSm[:, :], sums[pi][:, :], idb[0:16, 0:16]), inc=(sT, 1))
            sums_cons[pi] = sT.n
            p.w("vector", sT)
            p.op("vector", lambda e: e.tensor_copy(sumf[:, :], pSm[:, :]), inc=(sV, 1))
            pSm_ev[0] = sV.n
            p.w("vector", sV)
            p.op("vector", lambda e, pi=pi: e.reciprocal(rcp[pi][:, :], sumf[:, :]),
                 inc=(sV, 1))

        def p3_pre_x1(i):
            pi = i % 2
            x14 = i % 4
            p.w("vector", sV)
            p.w("vector", qPX[x14])
            for hh in range(H):
                p.op("vector", lambda e, hh=hh, pi=pi, x14=x14: e.scalar_tensor_tensor(
                    x1[x14][:, hh * 64:(hh + 1) * 64], pTc[:, hh * 64:(hh + 1) * 64],
                    rcp[pi][:, hh:hh + 1], px[x14][:, hh * 64:(hh + 1) * 64],
                    ALU.mult, ALU.add), inc=(sV, 1) if hh == H - 1 else None)
            px_cons[x14] = sV.n
            pTc_ev[0] = sV.n
            b_, tau_ = i // NT, i % NT
            p.w("gpsimd", sV, v=sV.n)
            p.op("gpsimd", lambda e, b_=b_, tau_=tau_, x14=x14: e.dma_start(
                out=out_d[b_, tau_ * 128:tau_ * 128 + 128, :], in_=x1[x14][:, :]),
                inc=(gX1[i % 2], 16))
            x1_st[i % 2] = gX1[i % 2].n
            p.op("vector", lambda e, x14=x14: e.reduce_sum(
                stat2[:, 0:1], x1[x14][:, :], axis=AX), inc=(sV, 1))
            p.w("scalar", sV)
            p.w("scalar", sV, v=last_var2[0])
            p.w("scalar", sT, v=h2t_done[pi])
            p.op("scalar", lambda e, x14=x14, pi=pi: e.activation(
                tb2[pi][:, :], x1[x14][:, :], AF.Square, accum_out=stat2[:, 1:2]),
                inc=(sS, 1))
            p.w("vector", sV)
            p.w("vector", sS)
            p.op("vector", lambda e: e.tensor_scalar(
                stat2[:, 2:3], stat2[:, 0:1], 1.0 / D, None, ALU.mult), inc=(sV, 1))
            p.w("vector", sV)
            p.op("vector", lambda e: e.tensor_mul(
                stat2[:, 3:4], stat2[:, 2:3], stat2[:, 2:3]), inc=(sV, 1))
            p.w("vector", sV)
            p.op("vector", lambda e: e.scalar_tensor_tensor(
                stat2[:, 4:5], stat2[:, 1:2], 1.0 / D, stat2[:, 3:4],
                ALU.mult, ALU.subtract), inc=(sV, 1))
            last_var2[0] = sV.n
            p.w("scalar", sV)
            p.op("scalar", lambda e: e.activation(
                stat2[:, 5:6], stat2[:, 4:5], AF.Ln, bias=1e-5), inc=(sS, 1))
            p.w("scalar", sS)
            p.op("scalar", lambda e: e.activation(
                stat2[:, 6:7], stat2[:, 5:6], AF.Exp, scale=-0.5), inc=(sS, 1))
            p.w("vector", sS)
            p.op("vector", lambda e, pi=pi, x14=x14: e.tensor_scalar(
                tb2[pi][:, :], x1[x14][:, :], stat2[:, 2:3], stat2[:, 6:7],
                ALU.subtract, ALU.mult), inc=(sV, 1))

        def p3_pre_h2T(i):
            b = i // NT
            pi = i % 2
            p.w("tensor", sV)
            p.w("tensor", sV, v=pTc_ev[0])
            for ic in range(8):
                p.op("tensor", lambda e, ic=ic, pi=pi: e.transpose(
                    pTc[:, ic * 128:(ic + 1) * 128], tb2[pi][:, ic * 128:(ic + 1) * 128],
                    idb[:, :]), inc=(sT, 1) if ic == 7 else None)
            h2t_done[pi] = sT.n
            p.w("vector", sT)
            p.w("vector", sT, v=h2_cons[pi])
            for ic in range(8):
                p.op("vector", lambda e, ic=ic, pi=pi, b=b: e.tensor_scalar(
                    h2fm[pi][:, ic * 128:(ic + 1) * 128], pTc[:, ic * 128:(ic + 1) * 128],
                    cb1[:, ic * 2 + b: ic * 2 + b + 1],
                    sb_f[:, ic * 2 + b: ic * 2 + b + 1],
                    ALU.mult, ALU.add), inc=(sV, 1) if ic == 7 else None)
            pTc_ev[0] = sV.n

        def p3_pre_gate(i):
            pi = i % 2
            p.w("tensor", sV)
            p.w("tensor", sV, v=pG_ev[0])
            for ic in range(8):
                p.op("tensor", lambda e, ic=ic, pi=pi: e.matmul(
                    pG[:, :], h2fm[pi][:, ic * 128:(ic + 1) * 128],
                    gw_sb[:, ic * 8:(ic + 1) * 8],
                    start=(ic == 0), stop=(ic == 7)), inc=(sT, 1) if ic == 7 else None)
            p.w("vector", sT)
            p.op("vector", lambda e: e.max(out=top8[:, :], in_=pG[:, :]), inc=(sV, 1))
            p.w("vector", sV)
            p.op("vector", lambda e: e.tensor_scalar(
                m1[:, :], pG[:, :], top8[:, 0:1], None, ALU.is_equal))
            p.op("vector", lambda e: e.tensor_scalar(
                m2[:, :], pG[:, :], top8[:, 1:2], None, ALU.is_equal))
            p.op("vector", lambda e: e.tensor_sub(
                wcols[:, 0:1], top8[:, 1:2], top8[:, 0:1]), inc=(sV, 1))
            pG_ev[0] = sV.n
            p.w("scalar", sV)
            p.op("scalar", lambda e: e.activation(
                wcols[:, 0:1], wcols[:, 0:1], AF.Exp), inc=(sS, 1))

        def p3_pre_wf(i):
            pi = i % 2
            p.w("vector", sS)
            p.op("vector", lambda e: e.tensor_scalar(
                wcols[:, 1:2], wcols[:, 0:1], 1.0, None, ALU.add), inc=(sV, 1))
            p.w("vector", sV)
            p.op("vector", lambda e: e.reciprocal(wcols[:, 1:2], wcols[:, 1:2]),
                 inc=(sV, 1))
            p.w("vector", sV)
            p.op("vector", lambda e: e.tensor_scalar(
                wcols[:, 2:3], wcols[:, 1:2], 1.0, -1.0, ALU.subtract, ALU.mult),
                inc=(sV, 1))
            p.w("vector", sV)
            p.op("vector", lambda e: e.tensor_scalar(
                m1[:, :], m1[:, :], wcols[:, 1:2], None, ALU.mult), inc=(sV, 1))
            p.w("vector", sV)
            p.op("vector", lambda e: e.scalar_tensor_tensor(
                wf[:, :], m2[:, :], wcols[:, 2:3], m1[:, :], ALU.mult, ALU.add),
                inc=(sV, 1))
            vwf = sV.n
            p.w("tensor", sV, v=vwf)
            p.w("tensor", sV, v=pWf_ev[0])
            p.op("tensor", lambda e: e.transpose(pWf[:, :], wf[:, :], idf[:, :]),
                 inc=(sT, 1))
            p.w("vector", sT)
            p.op("vector", lambda e: e.tensor_copy(wft[:, :], pWf[:, :]), inc=(sV, 1))
            pWf_ev[0] = sV.n
            vwft = sV.n
            pe0, pe1 = pE[2 * (i % 2)], pE[2 * (i % 2) + 1]
            p.w("tensor", sV, v=vwft)
            p.w("tensor", sV, v=pE_ev[i % 2])
            for e_ in range(8):
                pe = (pe0, pe1)[e_ // 4]
                p.op("tensor", lambda e, e_=e_, pe=pe: e.matmul(
                    pe[:, (e_ % 4) * 128:(e_ % 4 + 1) * 128],
                    oneh[:, e_ * 128:(e_ + 1) * 128], wft[:, :],
                    start=True, stop=True), inc=(sT, 1) if e_ == 7 else None)
            p.w("vector", sT)
            p.w("vector", sV, v=wrep_cons[pi])
            p.op("vector", lambda e, pi=pi, pe0=pe0: e.tensor_copy(
                wrep[pi][:, 0:512], pe0[:, :]))
            p.op("vector", lambda e, pi=pi, pe1=pe1: e.tensor_copy(
                wrep[pi][:, 512:1024], pe1[:, :]), inc=(sV, 1))
            wrep_rdy[pi] = sV.n

        PRE_UNITS = [p3_pre_loads, p3_pre_ctxT, p3_pre_x1, p3_pre_h2T,
                     p3_pre_gate, p3_pre_wf]

        def p3_prework(i):
            for u in PRE_UNITS:
                u(i)

        def p3_scaled(i):
            pi = i % 2
            for e_ in range(8):
                p.w("vector", sV, v=wrep_rdy[pi])
                p.w("vector", sT, v=scl_cons[e_])
                wap = bass.AP(wrep[pi], e_ * 128, [[8 * 128, 128], [0, 8], [1, 128]])
                p.op("vector", lambda e, e_=e_, pi=pi, wap=wap: e.tensor_tensor(
                    scl[e_][:, :], h2fm[pi][:, :], wap, ALU.mult), inc=(sV, 1))
                scl_done[(i, e_)] = sV.n

        def p3_experts(i, nxt):
            pi = i % 2
            pe0, pe1 = pE[2 * (i % 2)], pE[2 * (i % 2) + 1]
            for e_ in range(8):
                si = e_
                p.w("tensor", sV, v=scl_done[(i, e_)])
                if e_ == 0:
                    if i == 0:
                        p.w("tensor", qEW)
                    p.w("tensor", sV, v=pE_ev[i % 2])
                for ic in range(8):
                    for oh in range(2):
                        pe = (pe0, pe1)[oh]
                        p.op("tensor", lambda e, e_=e_, ic=ic, oh=oh, si=si, pe=pe: e.matmul(
                            pe[:, :], scl[si][:, ic * 128:(ic + 1) * 128],
                            ew_sb[:, e_ * 8 * D + ic * D + oh * 512: e_ * 8 * D + ic * D + oh * 512 + 512],
                            start=(e_ == 0 and ic == 0), stop=(e_ == 7 and ic == 7)),
                            inc=(sT, 1) if (ic == 7 and oh == 1) else None)
                scl_cons[si] = sT.n
                if nxt is not None:
                    if e_ == 0:
                        PRE_UNITS[0](nxt)
                        PRE_UNITS[1](nxt)
                    elif e_ <= 4:
                        PRE_UNITS[e_ + 1](nxt)
            h2_cons[pi] = sT.n
            wrep_cons[pi] = sV.n
            texp_by[i] = sT.n

        def p3_evac(i):
            b = i // NT
            t0 = (i % NT) * 128
            pi = i % 2
            pe0, pe1 = pE[2 * (i % 2)], pE[2 * (i % 2) + 1]
            p.w("vector", sT, v=texp_by[i])
            for oh in range(2):
                pe = (pe0, pe1)[oh]
                p.w("vector", gO[oh], v=tmpb_st[oh])
                p.op("vector", lambda e, oh=oh, b=b, pe=pe: e.tensor_tensor(
                    tmpb2[oh][:, :], pe[:, :],
                    gbrep[:, b * D + oh * 512: b * D + (oh + 1) * 512], ALU.mult),
                    inc=(sV, 1))
                p.w("gpsimd", sV, v=sV.n)
                p.w("gpsimd", gX1[pi], v=x1_st[pi])
                p.op("gpsimd", lambda e, oh=oh, b=b, t0=t0: e.dma_start(
                    out=out_d[b, t0:t0 + 128, oh * 512:(oh + 1) * 512],
                    in_=tmpb2[oh][:, :], accum_op=ALU.add), inc=(gO[oh], 16))
                tmpb_st[oh] = gO[oh].n
            pE_ev[i % 2] = sV.n

        NTI = NB * NT
        p3_prework(0)
        for i in range(NTI):
            p3_scaled(i)
            if i > 0:
                p3_evac(i - 1)
            p3_experts(i, i + 1 if i + 1 < NTI else None)
        p3_evac(NTI - 1)

        p.w("gpsimd", gO[0])
        p.w("gpsimd", gO[1])
        p.flush()
        es3.close()

    return nc



def _consts():
    import ml_dtypes
    bf = ml_dtypes.bfloat16
    idb = np.eye(128, dtype=bf)
    idf = np.eye(128, dtype=np.float32)
    kpad2 = np.zeros((2, 128), np.float32)
    kpad2[0, 0:64] = NEG
    kpad2[1, 64:128] = NEG
    qpat = np.zeros((2, 768), np.float32)
    qpat[0, 64:256] = 1.0
    qpat[1, 128:256] = 1.0
    qpat[0, 256 + 192:512] = 1.0
    qpat[1, 256:256 + 64] = 1.0
    qpat[0, 512:512 + 128] = 1.0
    qpat[1, 512:512 + 192] = 1.0
    kpadB = np.zeros((2, 384), np.float32)
    kpadB[0, 64:128] = NEG
    kpadB[0, 128:192] = NEG
    kpadB[1, 192:256] = NEG
    kpadB[1, 256:320] = NEG
    qindB = np.zeros((2, 128), np.float32)
    qindB[0, 64:128] = 1.0
    qindB[1, 0:64] = 1.0
    oneh = np.zeros((8, 8 * 128), np.float32)
    for e in range(8):
        oneh[e, e * 128:(e + 1) * 128] = 1.0
    return dict(idb=idb, idf=idf, kpad2=kpad2.astype(bf), kpadB=kpadB.astype(bf),
                qpat=qpat.astype(bf), qindB=qindB.astype(bf), oneh=oneh.astype(bf))


def _layout_weights(inp):
    f32 = np.float32
    Wq = np.asarray(inp["L0_attn_Wq"], f32)
    Wk = np.asarray(inp["L0_attn_Wk"], f32)
    Wv = np.asarray(inp["L0_attn_Wv"], f32)
    gw = np.asarray(inp["L0_ffn_gate_w"], f32)
    ew = np.asarray(inp["L0_ffn_expert_w"], f32)
    wmod = np.asarray(inp["L0_wmod"], f32)

    def wfm(WT):
        return np.ascontiguousarray(WT.reshape(8, 128, -1).transpose(1, 0, 2).reshape(128, -1))

    d = {}
    d["wq"] = wfm(np.ascontiguousarray(Wq.T))
    d["wk"] = wfm(np.ascontiguousarray(Wk.T))
    d["wv"] = wfm(np.ascontiguousarray(Wv.T))
    d["gw"] = wfm(np.ascontiguousarray(gw.T))
    d["ew"] = np.ascontiguousarray(np.stack([wfm(np.ascontiguousarray(ew[e].T))
                                             for e in range(8)]))
    d["wm"] = np.ascontiguousarray(
        wmod.reshape(8, 128, 48, 128).transpose(2, 1, 0, 3).reshape(48, 128, 1024))
    d["bq"] = np.ascontiguousarray(np.asarray(inp["L0_attn_bq"], f32).reshape(8, 128))
    d["bk"] = np.ascontiguousarray(np.asarray(inp["L0_attn_bk"], f32).reshape(8, 128))
    d["bv"] = np.ascontiguousarray(np.asarray(inp["L0_attn_bv"], f32).reshape(1, 1024))
    return d


def kernel(**inp):
    x = np.asarray(inp["L0_attn_hidden_states"], np.float32)
    c = np.asarray(inp["L0_c"], np.float32)
    shared = _layout_weights(inp)
    shared.update(_consts())
    nc = build()
    maps = []
    for ci in range(NCORES):
        m = dict(shared)
        m["x"] = np.ascontiguousarray(x[NB * ci:NB * (ci + 1)])
        m["c2"] = np.ascontiguousarray(c[NB * ci:NB * (ci + 1)])
        maps.append(m)
    res = run_bass_kernel_spmd(nc, maps, core_ids=list(range(NCORES)))
    return np.concatenate([np.asarray(r["out"]) for r in res.results], axis=0)
